# version 01
# baseline (speedup 1.0000x reference)
import numpy as np
import concourse.bass as bass
import concourse.mybir as mybir
from concourse.bass_utils import run_bass_kernel_spmd

F32 = mybir.dt.float32
BF16 = mybir.dt.bfloat16
U8 = mybir.dt.uint8
AF = mybir.ActivationFunctionType
ALU = mybir.AluOpType
AX = mybir.AxisListType

T = 4096
D = 1024
NB = 8
LN_EPS = 1e-5

_DT_SZ = {F32: 4, BF16: 2, U8: 1, mybir.dt.int32: 4}
_ENG_SEM = {"vector": "V", "scalar": "S", "gpsimd": "G"}


class SbufPool:

    def __init__(self, nc, reserve_tail=16384):
        size = nc.sbuf_bytes_remaining - reserve_tail
        size -= size % 32
        r = nc.bump_sbuf(size)
        assert r is not None
        self.nc = nc
        self.base = r[0]
        self.size = size
        self.free_list = [(0, size)]
        self.live = {}
        self.uid = 0

    def alloc(self, name, shape, dtype):
        bpp = int(np.prod(shape[1:])) * _DT_SZ[dtype]
        bpp = (bpp + 31) // 32 * 32
        for i, (s, ln) in enumerate(self.free_list):
            if ln >= bpp:
                self.free_list[i] = (s + bpp, ln - bpp)
                if self.free_list[i][1] == 0:
                    del self.free_list[i]
                self.uid += 1
                t = self.nc.alloc_sbuf_tensor_at(
                    f"{name}#{self.uid}", list(shape), dtype,
                    offset=self.base + s)
                self.live[id(t)] = (s, bpp)
                return t
        raise MemoryError(
            f"SbufPool: no space for {name} ({bpp}B); free={self.free_list}")

    def alloc_n(self, name, n, shape, dtype):
        return [self.alloc(f"{name}{i}", shape, dtype) for i in range(n)]

    def free(self, *tensors):
        for t in tensors:
            if isinstance(t, list):
                self.free(*t)
                continue
            s, ln = self.live.pop(id(t))
            self.free_list.append((s, ln))
        self.free_list.sort()
        merged = []
        for s, ln in self.free_list:
            if merged and merged[-1][0] + merged[-1][1] == s:
                merged[-1] = (merged[-1][0], merged[-1][1] + ln)
            else:
                merged.append((s, ln))
        self.free_list = [tuple(x) for x in merged]


class Prog:
    ENGINES = ("tensor", "vector", "scalar", "gpsimd", "sync")

    def __init__(self, nc):
        self.nc = nc
        self.ops = {e: [] for e in self.ENGINES}
        self._sems = {}
        self.cnt = {}
        self.pb = [nc.alloc_psum_tensor(f"pb{i}", [128, 512], F32)
                   for i in range(NB)]
        self.bank_free = [None] * NB
        self.pool = SbufPool(nc)
        self._free_q = []
        for s in ("V", "S", "G", "Tm"):
            self.sem(s)

    def sem(self, name):
        if name not in self._sems:
            self._sems[name] = self.nc.alloc_semaphore(name)
            self.cnt[name] = 0
        return self._sems[name]

    def emit(self, eng, build, waits=(), tinc=False):
        waits = [(self.sem(n), c) for n, c in waits if c > 0]
        ret = None
        es = _ENG_SEM.get(eng)
        if es is not None:
            self.cnt[es] += 1
            ret = (es, self.cnt[es])
            inc = (self._sems[es], 1)
        elif tinc:
            assert eng == "tensor"
            self.cnt["Tm"] += 1
            ret = ("Tm", self.cnt["Tm"])
            inc = (self._sems["Tm"], 1)
        else:
            inc = None

        def f(e):
            for s, c in waits:
                e.wait_ge(s, c)
            ins = build(e)
            if inc is not None:
                ins.then_inc(*inc)
        self.ops[eng].append(f)
        return ret

    def emit_wait(self, eng, waits):
        waits = [(self.sem(n), c) for n, c in waits if c > 0]

        def f(e):
            for s, c in waits:
                e.wait_ge(s, c)
        self.ops[eng].append(f)

    def dma(self, queue, out, in_, sem, waits=(), accum=False):
        self.sem(sem)
        waits = tuple(waits) + ((sem, self.cnt[sem]),)
        waits = [(self._sems[n], c) for n, c in
                 [(n, c) for n, c in waits] if c > 0]
        self.cnt[sem] += 16
        tgt = self.cnt[sem]

        def f(e):
            for s, c in waits:
                e.wait_ge(s, c)
            kw = {"accum_op": ALU.add} if accum else {}
            e.dma_start(out=out, in_=in_, **kw).then_inc(self._sems[sem], 16)
        self.ops[queue].append(f)
        return (sem, tgt)

    def free_later(self, *tensors):
        self._free_q.extend(tensors)

    def barrier(self):
        waits = [(n, c) for n, c in self.cnt.items() if c > 0]
        for eng in self.ENGINES:
            self.emit_wait(eng, waits)
        if self._free_q:
            self.pool.free(*self._free_q)
            self._free_q = []
        self.bank_free = [None] * NB

    def bank(self, i, parts=128, width=512):
        return self.pb[i][0:parts, 0:width]

    def use_bank(self, i):
        w = self.bank_free[i]
        return (w,) if w else ()

    def done_bank(self, i, pred):
        self.bank_free[i] = pred

    def finalize(self):
        with self.nc.Block() as block:
            for eng in self.ENGINES:
                fns = self.ops[eng]
                getattr(block, eng)(lambda e, fns=fns: [f(e) for f in fns])


def colform(v):
    v = np.asarray(v, np.float32)
    return np.ascontiguousarray(v.reshape(-1, 128).T)


def trows(a):
    a = np.asarray(a, np.float32)
    return np.ascontiguousarray(a.reshape(-1, 128, a.shape[-1]))


CONST_VALS = (1e-5, 1e-6, -0.5, 0.5, -1.0, 2.0, -2.0, 0.125, -0.125,
              1.0 / 64, -1.0 / 64, 1e-4, -1e6, -1e4, 1e6, 1e4, 0.0625)


class Consts:
    def __init__(self, p):
        nc = p.nc
        self.ones_col_bf = p.pool.alloc("c_onecol", [128, 1], BF16)
        self.ones_row_f = p.pool.alloc("c_onerow", [1, 128], F32)
        p.emit("vector", lambda e: e.memset(self.ones_col_bf[:], 1.0))
        p.emit("vector", lambda e: e.memset(self.ones_row_f[:], 1.0))
        for v in CONST_VALS:
            t = p.pool.alloc(f"c_{v}", [128, 1], F32)
            p.emit("vector", lambda e, t=t, v=v: e.memset(t[:], float(v)))
            nc.const_aps.register(F32, float(v), t[:])


class RowStream:
    _uid = 0

    def __init__(self, p, dram_row_fn, width, dtype, queue="sync", depth=3,
                 name=None, nrows=None, sem=None):
        RowStream._uid += 1
        self.p, self.queue, self.depth = p, queue, depth
        self.name = name or f"rs{RowStream._uid}"
        self.bufs = p.pool.alloc_n(self.name, depth, [128, width], dtype)
        self.row_fn = dram_row_fn
        self.ld = sem or (self.name + "_ld")
        p.sem(self.ld)
        self.base = p.cnt[self.ld]
        self.issued = 0
        self.nrows = nrows
        self.freed = []

    def issue(self, upto):
        if self.nrows is not None:
            upto = min(upto, self.nrows - 1)
        while self.issued <= upto:
            j = self.issued
            waits = ()
            if j >= self.depth:
                assert len(self.freed) >= j - self.depth + 1, \
                    f"{self.name}: row {j - self.depth} not freed yet"
                waits = self.freed[j - self.depth]
            self.p.dma(self.queue, self.bufs[j % self.depth][:],
                       self.row_fn(j), self.ld, waits=waits)
            self.issued += 1

    def get(self, k, ahead=1):
        self.issue(k + ahead)
        return (self.bufs[k % self.depth],
                ((self.ld, self.base + 16 * (k + 1)),))

    def freed_when(self, *preds):
        self.freed.append(tuple(preds))

    def done(self):
        self.p.free_later(*self.bufs)


def ln_stats_stream(p, name, consts, KT, get_row, a1, a2, stream=None,
                    width=T, src_bf16=False):
    xb = p.pool.alloc(name + "_xb", [128, width], BF16)
    sq = p.pool.alloc(name + "_sq", [128, width], BF16)
    m1 = p.emit("vector", lambda e: e.memset(a1[:], 0.0))
    m2 = p.emit("vector", lambda e: e.memset(a2[:], 0.0))
    nchunk = width // 512
    last_add = m2
    last_mm = None
    for k in range(KT):
        xrow, w = get_row(k)
        wcast = tuple(w)
        if last_mm is not None:
            wcast = wcast + (last_mm,)
        if src_bf16:
            xb_use = xrow
            csq = p.emit("scalar", lambda e, xrow=xrow: e.activation(
                sq[:], xrow[:], AF.Square), waits=wcast)
            casts = (csq,)
        else:
            xb_use = xb
            cxb = p.emit("vector", lambda e, xrow=xrow: e.tensor_copy(
                xb[:], xrow[:]), waits=wcast)
            csq = p.emit("scalar", lambda e, xrow=xrow: e.activation(
                sq[:], xrow[:], AF.Square), waits=wcast)
            casts = (cxb, csq)
        if stream and not src_bf16:
            stream.freed_when(*casts)
        for t in range(nchunk):
            b6 = p.bank(6, 1)
            b7 = p.bank(7, 1)
            wb = ((last_add,) + tuple(casts) + p.use_bank(6) + p.use_bank(7))
            p.emit("tensor",
                   lambda e, t=t, b6=b6, xb_use=xb_use: e.matmul(
                       b6, consts.ones_col_bf[:],
                       xb_use[:, t * 512:(t + 1) * 512],
                       start=True, stop=True),
                   waits=wb)
            last_mm = p.emit("tensor",
                             lambda e, t=t, b7=b7: e.matmul(
                                 b7, consts.ones_col_bf[:],
                                 sq[:, t * 512:(t + 1) * 512],
                                 start=True, stop=True), tinc=True)
            a1a = p.emit("vector",
                         lambda e, t=t, b6=b6: e.tensor_add(
                             a1[:, t * 512:(t + 1) * 512],
                             a1[:, t * 512:(t + 1) * 512], b6),
                         waits=(last_mm, (last_add[0], last_add[1])))
            last_add = p.emit("vector",
                              lambda e, t=t, b7=b7: e.tensor_add(
                                  a2[:, t * 512:(t + 1) * 512],
                                  a2[:, t * 512:(t + 1) * 512], b7),
                              waits=(a1a,))
            p.done_bank(6, last_add)
            p.done_bank(7, last_add)
        if stream and src_bf16:
            stream.freed_when(last_mm)
    p.free_later(xb, sq)
    return last_add


def ln_rows(p, name, a1, a2, n_feat, waits, mu, r, var, tm, eps=LN_EPS):
    inv = 1.0 / n_feat
    c1 = p.emit("vector", lambda e: e.tensor_scalar_mul(mu[:], a1[:], inv),
                waits=waits)
    c2 = p.emit("vector", lambda e: e.tensor_scalar_mul(var[:], a2[:], inv),
                waits=waits)
    c3 = p.emit("vector", lambda e: e.tensor_mul(tm[:], mu[:], mu[:]),
                waits=(c1,))
    c4 = p.emit("vector", lambda e: e.tensor_sub(var[:], var[:], tm[:]),
                waits=(c2, c3))
    c5 = p.emit("scalar", lambda e: e.activation(tm[:], var[:], AF.Ln,
                                                 bias=eps), waits=(c4,))
    c6 = p.emit("scalar", lambda e: e.activation(r[:], tm[:], AF.Exp,
                                                 scale=-0.5), waits=(c5,))
    return c6


def bcast_row(p, name, row, out, consts, waits, width=T):
    nchunk = width // 512
    last = None
    for t in range(nchunk):
        bi = 6 + (t % 2)
        b = p.bank(bi)
        w = tuple(waits) + p.use_bank(bi)
        mm = p.emit("tensor",
                    lambda e, row=row, t=t, b=b: e.matmul(
                        b, consts.ones_row_f[:],
                        row[:, t * 512:(t + 1) * 512],
                        start=True, stop=True),
                    waits=w, tinc=True)
        last = p.emit("vector",
                      lambda e, t=t, b=b: e.tensor_copy(
                          out[:, t * 512:(t + 1) * 512], b),
                      waits=(mm,))
        p.done_bank(bi, last)
    return last


def quantize_ternary(p, name, w_rows, KT, width, consts, q_rows_out):
    st = RowStream(p, w_rows, width, F32, queue="sync", depth=2,
                   name=name + "_wa", nrows=KT)
    acc = p.pool.alloc(name + "_acc", [128, KT], F32)
    red = None
    for k in range(KT):
        buf, w = st.get(k)
        red = p.emit("vector",
                     lambda e, buf=buf, k=k: e.tensor_reduce(
                         acc[:, k:k + 1], buf[:], AX.X, ALU.add,
                         apply_absolute_value=True),
                     waits=w)
        st.freed_when(red)
    st.done()
    col = p.pool.alloc(name + "_col", [128, 1], BF16)
    colf = p.pool.alloc(name + "_colf", [128, 1], F32)
    c1 = p.emit("vector", lambda e: e.tensor_reduce(colf[:], acc[:], AX.X,
                                                    ALU.add), waits=(red,))
    c2 = p.emit("vector", lambda e: e.tensor_copy(col[:], colf[:]),
                waits=(c1,))
    b = p.bank(6, 1, 1)
    mm1 = p.emit("tensor", lambda e: e.matmul(b, consts.ones_col_bf[:],
                                              col[:], start=True, stop=True),
                 waits=(c2,) + p.use_bank(6), tinc=True)
    tot = p.pool.alloc(name + "_tot", [1, 8], F32)
    scal = p.pool.alloc(name + "_scl", [1, 8], F32)
    n_el = KT * 128 * width
    z1 = p.emit("vector", lambda e: e.memset(tot[:], 0.0))
    z2 = p.emit("vector", lambda e: e.memset(scal[:], 0.0))
    c3 = p.emit("vector", lambda e: e.tensor_copy(tot[:, 0:1], b),
                waits=(mm1, z1, z2))
    c4 = p.emit("vector", lambda e: e.tensor_scalar_mul(tot[:], tot[:],
                                                        1.0 / n_el),
                waits=(c3,))
    c5 = p.emit("vector", lambda e: e.tensor_scalar_max(tot[:], tot[:], 1e-5),
                waits=(c4,))
    p.done_bank(6, c3)
    c6 = p.emit("vector", lambda e: e.reciprocal(scal[:], tot[:]),
                waits=(c5,))
    rs_col = p.pool.alloc(name + "_rcol", [128, 1], F32)
    scol = p.pool.alloc(name + "_scol", [128, 1], F32)
    b2 = p.bank(7, 128, 1)
    mm2 = p.emit("tensor", lambda e: e.matmul(b2, consts.ones_row_f[:],
                                              scal[:, 0:1], start=True,
                                              stop=True),
                 waits=(c6,) + p.use_bank(7), tinc=True)
    c7 = p.emit("vector", lambda e: e.tensor_copy(scol[:], b2), waits=(mm2,))
    p.done_bank(7, c7)
    b3 = p.bank(6, 128, 1)
    mm3 = p.emit("tensor", lambda e: e.matmul(b3, consts.ones_row_f[:],
                                              tot[:, 0:1], start=True,
                                              stop=True),
                 waits=(c5,) + p.use_bank(6), tinc=True)
    c8 = p.emit("vector", lambda e: e.tensor_copy(rs_col[:], b3), waits=(mm3,))
    p.done_bank(6, c8)
    st2 = RowStream(p, w_rows, width, F32, queue="sync", depth=2,
                    name=name + "_wb", nrows=KT)
    tf = p.pool.alloc(name + "_tf", [128, width], F32)
    sg = p.pool.alloc(name + "_sg", [128, width], F32)
    qb = p.pool.alloc_n(name + "_qb", 2, [128, width], BF16)
    store = None
    qprev = None
    for k in range(KT):
        buf, w = st2.get(k)
        w = tuple(w) + (c7,)
        if qprev is not None:
            w = w + (qprev,)
        u1 = p.emit("vector",
                    lambda e, buf=buf: e.tensor_scalar_mul(tf[:], buf[:],
                                                           scol[:]),
                    waits=w)
        st2.freed_when(u1)
        u2 = p.emit("scalar", lambda e: e.activation(sg[:], tf[:], AF.Sign),
                    waits=(u1,))
        u3 = p.emit("scalar", lambda e: e.activation(tf[:], tf[:], AF.Abs),
                    waits=(u2,))
        u4 = p.emit("vector", lambda e: e.tensor_single_scalar(
            tf[:], tf[:], 0.5, ALU.is_ge), waits=(u3,))
        wq = ((name + "_qst", 16 * (k - 1)),) if k >= 2 else ()
        qprev = p.emit("vector",
                       lambda e, k=k: e.tensor_mul(qb[k % 2][:], tf[:],
                                                   sg[:]),
                       waits=(u4,) + wq)
        store = p.dma("gpsimd", q_rows_out(k), qb[k % 2][:], name + "_qst",
                      waits=(qprev,))
    st2.done()
    p.free_later(acc, col, colf, tot, scal, scol, tf, sg, *qb)
    return rs_col, store


def proj_omajor(p, name, KT, OT, stat_fn, mov_fn, epil_fn, nbanks=6,
                TC=8, tc=512, bank0=0):
    last = None
    for o in range(OT):
        for t in range(TC):
            bi = bank0 + (o * TC + t) % nbanks
            ps = p.bank(bi, 128, tc)
            mm = None
            for k in range(KT):
                sap, sw = stat_fn(k, o)
                map_, mw = mov_fn(k, t)
                w = tuple(sw) + tuple(mw)
                if k == 0:
                    w = w + p.use_bank(bi)
                mm = p.emit("tensor",
                            lambda e, ps=ps, sap=sap, map_=map_, k=k, KT=KT:
                            e.matmul(ps, sap, map_, start=(k == 0),
                                     stop=(k == KT - 1)),
                            waits=w, tinc=(k == KT - 1))
            last = epil_fn(o, t, ps, mm)
            p.done_bank(bi, last)
    return last


def proj_kmajor(p, name, KT, OT, stat_fn, mov_fn, epil_fn, TC=8, tc=512):
    last = None
    for t in range(TC):
        gmm = None
        for k in range(KT):
            map_, mw, freed_cb = mov_fn(k, t)
            for o in range(OT):
                sap, sw = stat_fn(k, o)
                w = tuple(sw) + tuple(mw)
                if k == 0 and o == 0:
                    for oo in range(OT):
                        w = w + p.use_bank(oo)
                gmm = p.emit("tensor",
                             lambda e, ps=p.bank(o, 128, tc), sap=sap,
                             map_=map_, k=k, KT=KT:
                             e.matmul(ps, sap, map_, start=(k == 0),
                                      stop=(k == KT - 1)),
                             waits=w if o == 0 else (),
                             tinc=(o == OT - 1))
            if freed_cb is not None:
                freed_cb(gmm)
        for o in range(OT):
            last = epil_fn(o, t, p.bank(o, 128, tc), gmm)
            p.done_bank(o, last)
    return last


def ln_outer(p, name, consts, x_rows_fn, KT=8, width=T, queue="sync"):
    rows = p.pool.alloc_n(name + "_r", 4, [1, width], F32)
    a1, a2, mu0, r0 = rows
    tm = p.pool.alloc(name + "_tm", [1, width], F32)
    var = p.pool.alloc(name + "_va", [1, width], F32)
    xs = RowStream(p, x_rows_fn, width, F32, queue=queue, depth=2,
                   name=name + "_xs", nrows=KT)
    sp = ln_stats_stream(p, name + "_st", consts, KT,
                         lambda k: xs.get(k), a1, a2, stream=xs)
    xs.done()
    lp = ln_rows(p, name + "_lr", a1, a2, KT * 128, (sp,), mu0, r0, var, tm)
    mub = p.pool.alloc(name + "_mub", [128, width], F32)
    rb = p.pool.alloc(name + "_rb", [128, width], F32)
    b1 = bcast_row(p, name + "_bA", mu0, mub, consts, (lp,), width=width)
    b2 = bcast_row(p, name + "_bB", r0, rb, consts, (lp,), width=width)
    p.free_later(*rows, tm, var)
    return mub, rb, (b1, b2)


def stat_tiles(wT):
    wT = np.asarray(wT, np.float32)
    Din, Dout = wT.shape
    KT, OT = Din // 128, Dout // 128
    return np.ascontiguousarray(
        wT.reshape(KT, 128, OT, 128).transpose(2, 0, 1, 3))


class StatStream:

    def __init__(self, p, w_d, KT, OT, name, queue="gpsimd", depth=2,
                 sem=None):
        self.p, self.KT, self.OT, self.depth = p, KT, OT, depth
        self.name = name
        self.bufs = p.pool.alloc_n(name, depth, [128, KT * 128], BF16)
        self.w_d = w_d
        self.ld = sem or (name + "_ld")
        p.sem(self.ld)
        self.base = p.cnt[self.ld]
        self.issued = 0
        self.freed = []

    def issue(self, upto):
        upto = min(upto, self.OT - 1)
        while self.issued <= upto:
            j = self.issued
            waits = ()
            if j >= self.depth:
                waits = self.freed[j - self.depth]
            dst = self.bufs[j % self.depth][:].rearrange(
                "p (k o) -> p k o", o=128)
            src = self.w_d[j].rearrange("k p o -> p k o")
            self.p.dma("gpsimd", dst, src, self.ld, waits=waits)
            self.issued += 1

    def get(self, o, k):
        self.issue(o + 1)
        buf = self.bufs[o % self.depth]
        return (buf[:, k * 128:(k + 1) * 128],
                ((self.ld, self.base + 16 * (o + 1)),))

    def freed_when(self, *preds):
        self.freed.append(tuple(preds))

    def done(self):
        self.p.free_later(*self.bufs)


def build_l3_ffn(nc):
    p = Prog(nc)
    x_d = nc.dram_tensor("x", [8, 128, T], F32, kind="ExternalInput").ap()
    w1_d = nc.dram_tensor("w1T", [8, 128, 4096], F32,
                          kind="ExternalInput").ap()
    w2_d = nc.dram_tensor("w2T", [32, 128, 1024], F32,
                          kind="ExternalInput").ap()
    cols_in = {}
    for nm, w in (("ln1g", 8), ("ln1b", 8), ("b1", 32), ("ln2g", 32),
                  ("ln2b", 32), ("b2", 8)):
        cols_in[nm] = nc.dram_tensor(nm, [128, w], F32,
                                     kind="ExternalInput").ap()
    y_d = nc.dram_tensor("y", [8, 128, T], F32, kind="ExternalOutput").ap()

    w1q_d = nc.dram_tensor("w1q", [8, 128, 4096], BF16, kind="Internal").ap()
    w2q_d = nc.dram_tensor("w2q", [32, 128, 1024], BF16, kind="Internal").ap()
    h4_d = nc.dram_tensor("h4", [32, 128, T], BF16, kind="Internal").ap()
    h5_d = nc.dram_tensor("h5", [32, 128, T], BF16, kind="Internal").ap()

    consts = Consts(p)
    p.barrier()
    sv = {}
    cols_pred = None
    for nm, w in (("ln1g", 8), ("ln1b", 8), ("b1", 32), ("ln2g", 32),
                  ("ln2b", 32), ("b2", 8)):
        sv[nm] = p.pool.alloc("sv_" + nm, [128, w], F32)
        cols_pred = p.dma("sync", sv[nm][:], cols_in[nm][:], "cols_ld")

    rs1, q1p = quantize_ternary(p, "q1", lambda k: w1_d[k], 8, 4096, consts,
                                lambda k: w1q_d[k])
    p.barrier()
    rs2, q2p = quantize_ternary(p, "q2", lambda k: w2_d[k], 32, 1024, consts,
                                lambda k: w2q_d[k])
    p.barrier()

    rows = p.pool.alloc_n("r", 6, [1, T], F32)
    a1, a2, mu0, r0, var0, tm = rows
    xs = RowStream(p, lambda k: x_d[k], T, F32, queue="sync", depth=2,
                   name="xs0", nrows=8)
    sp = ln_stats_stream(p, "st0", consts, 8, lambda k: xs.get(k), a1, a2,
                         stream=xs)
    xs.done()
    lp = ln_rows(p, "r0", a1, a2, D, (sp,), mu0, r0, var0, tm)
    s01 = a1
    d1 = p.emit("vector", lambda e: e.tensor_mul(tm[:], r0[:], r0[:]),
                waits=(lp,))
    d2 = p.emit("vector", lambda e: e.tensor_mul(tm[:], tm[:], var0[:]),
                waits=(d1,))
    d3 = p.emit("scalar", lambda e: e.activation(tm[:], tm[:], AF.Ln,
                                                 bias=LN_EPS), waits=(d2,))
    d4 = p.emit("scalar", lambda e: e.activation(tm[:], tm[:], AF.Exp,
                                                 scale=-0.5), waits=(d3,))
    d5 = p.emit("vector", lambda e: e.tensor_mul(s01[:], r0[:], tm[:]),
                waits=(d4,))

    mub = p.pool.alloc("mub", [128, T], F32)
    rb = p.pool.alloc("rb", [128, T], F32)
    bc1 = bcast_row(p, "bcA", mu0, mub, consts, (lp,))
    bc2 = bcast_row(p, "bcB", s01, rb, consts, (d5,))
    p.free_later(*rows)
    p.barrier()

    h2 = p.pool.alloc_n("h2_", 8, [128, T], BF16)
    xs2 = RowStream(p, lambda k: x_d[k], T, F32, queue="sync", depth=2,
                    name="xs2", nrows=8)
    tmt = p.pool.alloc("tmt", [128, T], F32)
    h2p = None
    for k in range(8):
        buf, w = xs2.get(k)
        w = tuple(w) + (bc1, bc2, cols_pred)
        if h2p is not None:
            w = w + (h2p,)
        e1 = p.emit("vector", lambda e, buf=buf: e.tensor_sub(
            tmt[:], buf[:], mub[:]), waits=w)
        xs2.freed_when(e1)
        e2 = p.emit("vector", lambda e: e.tensor_mul(tmt[:], tmt[:], rb[:]),
                    waits=(e1,))
        h2p = p.emit("vector", lambda e, k=k: e.tensor_scalar(
            h2[k][:], tmt[:], sv["ln1g"][:, k:k + 1], sv["ln1b"][:, k:k + 1],
            ALU.mult, ALU.add), waits=(e2,))
    xs2.done()
    p.free_later(tmt)
    p.barrier()

    w1q = p.pool.alloc_n("w1q_", 8, [128, 4096], BF16)
    w1ld = None
    for k in range(8):
        w1ld = p.dma("sync", w1q[k][:], w1q_d[k], "w1q_ld", waits=(q1p,))
    ob = p.pool.alloc_n("p1o_", 2, [128, T], BF16)
    h4_store = None
    ostore = {}
    for o in range(32):
        evac = None
        for t in range(8):
            bi = (o * 8 + t) % 6
            ps = p.bank(bi)
            w = p.use_bank(bi) + (w1ld, h2p)
            if t == 0 and o >= 2:
                w = w + (ostore[o - 2],)
            mm = None
            for k in range(8):
                mm = p.emit("tensor",
                            lambda e, ps=ps, k=k, o=o, t=t: e.matmul(
                                ps, w1q[k][:, o * 128:(o + 1) * 128],
                                h2[k][:, t * 512:(t + 1) * 512],
                                start=(k == 0), stop=(k == 7)),
                            waits=w if k == 0 else (), tinc=(k == 7))
            evac = p.emit("scalar",
                          lambda e, ps=ps, o=o, t=t: e.activation(
                              ob[o % 2][:, t * 512:(t + 1) * 512], ps,
                              AF.Gelu, bias=sv["b1"][:, o:o + 1],
                              scale=rs1[:]),
                          waits=(mm,))
            p.done_bank(bi, evac)
        ostore[o] = p.dma("gpsimd", h4_d[o], ob[o % 2][:], "h4_st",
                          waits=(evac,))
    h4_store = ostore[31]
    p.free_later(*w1q, *h2, *ob)
    p.barrier()
    rows = p.pool.alloc_n("r2_", 6, [1, T], F32)
    tmt = p.pool.alloc("tmt2", [128, T], F32)

    h4s = RowStream(p, lambda k: h4_d[k], T, BF16, queue="sync", depth=2,
                    name="h4s", nrows=32)

    def h4row(k):
        buf, w = h4s.get(k)
        return buf, tuple(w) + (h4_store,)
    a1b, a2b, mu2, r2, varx, tmx = rows
    sp2 = ln_stats_stream(p, "st2", consts, 32, h4row, a1b, a2b, stream=h4s,
                          src_bf16=True)
    h4s.done()
    lp2 = ln_rows(p, "r2", a1b, a2b, 4096, (sp2,), mu2, r2, varx, tmx)
    bc3 = bcast_row(p, "bcC", mu2, mub, consts, (lp2,))
    bc4 = bcast_row(p, "bcD", r2, rb, consts, (lp2,))
    p.free_later(*rows)
    p.barrier()

    h4s2 = RowStream(p, lambda k: h4_d[k], T, BF16, queue="sync", depth=2,
                     name="h4s2", nrows=32)
    h5b = p.pool.alloc_n("h5b_", 2, [128, T], BF16)
    h5store = {}
    prev = None
    for k in range(32):
        buf, w = h4s2.get(k)
        w = tuple(w) + (bc3, bc4)
        if prev is not None:
            w = w + (prev,)
        if k >= 2:
            w = w + (h5store[k - 2],)
        e1 = p.emit("vector", lambda e, buf=buf: e.tensor_sub(
            tmt[:], buf[:], mub[:]), waits=w)
        h4s2.freed_when(e1)
        e2 = p.emit("vector", lambda e: e.tensor_mul(tmt[:], tmt[:], rb[:]),
                    waits=(e1,))
        prev = p.emit("vector", lambda e, k=k: e.tensor_scalar(
            h5b[k % 2][:], tmt[:], sv["ln2g"][:, k:k + 1],
            sv["ln2b"][:, k:k + 1], ALU.mult, ALU.add), waits=(e2,))
        h5store[k] = p.dma("gpsimd", h5_d[k], h5b[k % 2][:], "h5_st",
                           waits=(prev,))
    h5_done = h5store[31]
    h4s2.done()
    p.free_later(*h5b, mub, rb, tmt)
    p.barrier()

    w2q = p.pool.alloc_n("w2q_", 32, [128, 1024], BF16)
    w2ld = None
    for k in range(32):
        w2ld = p.dma("sync", w2q[k][:], w2q_d[k], "w2q_ld", waits=(q2p,))
    mvs = RowStream(p, lambda i: h5_d[i % 32][:, (i // 32) * 512:
                                              (i // 32 + 1) * 512],
                    512, BF16, queue="sync", depth=4, name="p2mv", nrows=256)
    xs3 = RowStream(p, lambda i: x_d[i % 8][:, (i // 8) * 512:
                                            (i // 8 + 1) * 512],
                    512, F32, queue="gpsimd", depth=4, name="xs3", nrows=64)
    yb = p.pool.alloc_n("yb_", 4, [128, 512], F32)
    ystore = {}
    for t in range(8):
        gmm = None
        for k in range(32):
            mi = t * 32 + k
            buf, w = mvs.get(mi)
            w = tuple(w) + (w2ld, h5_done)
            if k == 0:
                for o in range(8):
                    w = w + p.use_bank(o)
            for o in range(8):
                gmm = p.emit("tensor",
                             lambda e, k=k, o=o, buf=buf: e.matmul(
                                 p.bank(o), w2q[k][:, o * 128:(o + 1) * 128],
                                 buf[:], start=(k == 0), stop=(k == 31)),
                             waits=w if o == 0 else (), tinc=(o == 7))
            mvs.freed_when(gmm)
        for o in range(8):
            i = t * 8 + o
            xbuf, xw = xs3.get(i)
            yi = i % 4
            w = (gmm,) + tuple(xw) + (cols_pred,)
            if i >= 4:
                w = w + (ystore[i - 4],)
            f1 = p.emit("vector",
                        lambda e, o=o, yi=yi: e.tensor_scalar(
                            yb[yi][:], p.bank(o), rs2[:],
                            sv["b2"][:, o:o + 1], ALU.mult, ALU.add),
                        waits=w)
            f2 = p.emit("vector",
                        lambda e, yi=yi, xbuf=xbuf: e.tensor_add(
                            yb[yi][:], yb[yi][:], xbuf[:]),
                        waits=(f1,))
            xs3.freed_when(f2)
            p.done_bank(o, f2)
            ystore[i] = p.dma("sync", y_d[o][:, t * 512:(t + 1) * 512],
                              yb[yi][:], "y_st", waits=(f2,))
    p.emit_wait("sync", (ystore[63],))
    p.emit_wait("gpsimd", (ystore[63],))
    p.finalize()
    return nc


def build_l0_ffn(nc):
    p = Prog(nc)
    x_d = nc.dram_tensor("x", [8, 128, T], F32, kind="ExternalInput").ap()
    w1_d = nc.dram_tensor("w1st", [32, 8, 128, 128], F32,
                          kind="ExternalInput").ap()
    w2_d = nc.dram_tensor("w2st", [32, 8, 128, 128], F32,
                          kind="ExternalInput").ap()
    h1_d = nc.dram_tensor("h1st", [32, 32, 128, 128], F32,
                          kind="ExternalInput").ap()
    h2_d = nc.dram_tensor("h2st", [32, 32, 128, 128], F32,
                          kind="ExternalInput").ap()
    w3_d = nc.dram_tensor("w3T", [32, 128, 1024], F32,
                          kind="ExternalInput").ap()
    cols_in = {}
    for nm, w in (("s1", 32), ("s2", 32), ("sh1", 32), ("sh2", 32),
                  ("s3", 8)):
        cols_in[nm] = nc.dram_tensor(nm, [128, w], F32,
                                     kind="ExternalInput").ap()
    y_d = nc.dram_tensor("y", [8, 128, T], F32, kind="ExternalOutput").ap()

    ha_d = nc.dram_tensor("ha", [32, 128, T], BF16, kind="Internal").ap()
    hb_d = nc.dram_tensor("hb", [32, 128, T], BF16, kind="Internal").ap()
    hc_d = nc.dram_tensor("hc", [32, 128, T], BF16, kind="Internal").ap()
    h1b_d = nc.dram_tensor("h1b", [32, 32, 128, 128], BF16,
                           kind="Internal").ap()
    h2b_d = nc.dram_tensor("h2b", [32, 32, 128, 128], BF16,
                           kind="Internal").ap()

    consts = Consts(p)
    p.barrier()
    sv = {}
    cols_pred = None
    for nm, w in (("s1", 32), ("s2", 32), ("sh1", 32), ("sh2", 32),
                  ("s3", 8)):
        sv[nm] = p.pool.alloc("sv_" + nm, [128, w], F32)
        cols_pred = p.dma("sync", sv[nm][:], cols_in[nm][:], "cols_ld")
    pc = None
    for o in range(32):
        pc = p.dma("gpsimd", h1b_d[o], h1_d[o], "pc_ld")
        pc = p.dma("gpsimd", h2b_d[o], h2_d[o], "pc_ld")

    mub, rb, (bp1, bp2) = ln_outer(p, "ln0", consts, lambda k: x_d[k])
    p.barrier()

    u = p.pool.alloc_n("u_", 8, [128, T], BF16)
    xs2 = RowStream(p, lambda k: x_d[k], T, F32, queue="sync", depth=2,
                    name="uxs", nrows=8)
    tmt = p.pool.alloc("tmt", [128, T], F32)
    up = None
    for k in range(8):
        buf, w = xs2.get(k)
        w = tuple(w) + (bp1, bp2)
        if up is not None:
            w = w + (up,)
        e1 = p.emit("vector", lambda e, buf=buf: e.tensor_sub(
            tmt[:], buf[:], mub[:]), waits=w)
        xs2.freed_when(e1)
        up = p.emit("vector", lambda e, k=k: e.tensor_mul(u[k][:], tmt[:],
                                                          rb[:]), waits=(e1,))
    xs2.done()
    p.free_later(tmt, mub, rb)
    p.barrier()

    stA = StatStream(p, w1_d, 8, 32, "stA")
    stB = StatStream(p, w2_d, 8, 32, "stB")
    tfa = p.pool.alloc_n("tfa", 2, [128, 512], F32)
    tfb = p.pool.alloc_n("tfb", 2, [128, 512], F32)
    ob = p.pool.alloc_n("oba", 2, [128, T], BF16)
    hast = {}
    evhist = {}
    for o in range(32):
        lastmm = None
        evac = None
        for t in range(8):
            it = o * 8 + t
            biA = it % 3
            biB = 3 + it % 3
            psA = p.bank(biA)
            psB = p.bank(biB)
            for k in range(8):
                sap, sw = stA.get(o, k)
                w = tuple(sw) + (up,)
                if k == 0:
                    w = w + p.use_bank(biA)
                    if t == 0 and o >= 2:
                        w = w + (hast[o - 2],)
                p.emit("tensor",
                       lambda e, psA=psA, sap=sap, k=k, o=o, t=t: e.matmul(
                           psA, sap, u[k][:, t * 512:(t + 1) * 512],
                           start=(k == 0), stop=(k == 7)),
                       waits=w if k == 0 else ())
            for k in range(8):
                sap, sw = stB.get(o, k)
                w = tuple(sw) + (p.use_bank(biB) if k == 0 else ())
                lastmm = p.emit("tensor",
                                lambda e, psB=psB, sap=sap, k=k, o=o, t=t:
                                e.matmul(psB, sap,
                                         u[k][:, t * 512:(t + 1) * 512],
                                         start=(k == 0), stop=(k == 7)),
                                waits=w if k == 0 else (), tinc=(k == 7))
            ti = it % 2
            wtf = (lastmm, cols_pred)
            if it >= 2:
                wtf = wtf + (evhist[it - 2],)
            a1_ = p.emit("scalar",
                         lambda e, psA=psA, o=o, ti=ti: e.activation(
                             tfa[ti][:], psA, AF.Silu,
                             scale=sv["s1"][:, o:o + 1]),
                         waits=wtf)
            a2_ = p.emit("vector",
                         lambda e, psB=psB, o=o, ti=ti: e.tensor_scalar_mul(
                             tfb[ti][:], psB, sv["s2"][:, o:o + 1]),
                         waits=wtf)
            evac = p.emit("vector",
                          lambda e, o=o, t=t, ti=ti: e.tensor_mul(
                              ob[o % 2][:, t * 512:(t + 1) * 512],
                              tfa[ti][:], tfb[ti][:]),
                          waits=(a1_, a2_))
            evhist[it] = evac
            p.done_bank(biA, evac)
            p.done_bank(biB, evac)
        stA.freed_when(lastmm)
        stB.freed_when(lastmm)
        hast[o] = p.dma("sync", ha_d[o], ob[o % 2][:], "ha_st",
                        waits=(evac,))
    stA.done()
    stB.done()
    p.free_later(*tfa, *tfb, *ob, *u)
    p.barrier()

    def big_layer(src_d, wst_d, scol, dst_d, tag):
        dst_store = {}
        for th in range(2):
            mv = p.pool.alloc_n(tag + "mv", 32, [128, 2048], BF16)
            mvp = {}
            for k in range(32):
                mvp[k] = p.dma("sync", mv[k][:],
                               src_d[k][:, th * 2048:(th + 1) * 2048],
                               tag + "mv_ld")
            st = StatStream(p, wst_d, 32, 32, tag + f"st{th}")
            obx = p.pool.alloc_n(tag + "ob", 2, [128, 2048], BF16)
            lastmm = None
            for o in range(32):
                evac = None
                for t in range(4):
                    it = o * 4 + t
                    bi = it % 6
                    ps = p.bank(bi)
                    for k in range(32):
                        sap, sw = st.get(o, k)
                        w = ()
                        if k == 0:
                            w = w + tuple(sw) + p.use_bank(bi)
                            if t == 0 and o >= 2:
                                w = w + (dst_store[(th, o - 2)],)
                        if o == 0 and t == 0:
                            w = w + (mvp[k],)
                        lastmm = p.emit(
                            "tensor",
                            lambda e, ps=ps, sap=sap, k=k, o=o, t=t:
                            e.matmul(ps, sap,
                                     mv[k][:, t * 512:(t + 1) * 512],
                                     start=(k == 0), stop=(k == 31)),
                            waits=w, tinc=(k == 31))
                    evac = p.emit(
                        "scalar",
                        lambda e, ps=ps, o=o, t=t: e.activation(
                            obx[o % 2][:, t * 512:(t + 1) * 512], ps,
                            AF.Gelu, scale=scol[:, o:o + 1]),
                        waits=(lastmm, cols_pred))
                    p.done_bank(bi, evac)
                st.freed_when(lastmm)
                dst_store[(th, o)] = p.dma(
                    "sync", dst_d[o][:, th * 2048:(th + 1) * 2048],
                    obx[o % 2][:], tag + "_st", waits=(evac,))
            st.done()
            p.free_later(*mv, *obx)
            p.barrier()
        return dst_store[(1, 31)]

    hb_done = big_layer(ha_d, h1b_d, sv["sh1"], hb_d, "pB")
    hc_done = big_layer(hb_d, h2b_d, sv["sh2"], hc_d, "pC")

    w3q = p.pool.alloc_n("w3q_", 32, [128, 1024], BF16)
    w3ld = None
    for k in range(32):
        w3ld = p.dma("gpsimd", w3q[k][:], w3_d[k], "w3_ld")
    mvs = RowStream(p, lambda i: hc_d[i % 32][:, (i // 32) * 512:
                                              (i // 32 + 1) * 512],
                    512, BF16, queue="sync", depth=4, name="pDmv", nrows=256)
    xs3 = RowStream(p, lambda i: x_d[i % 8][:, (i // 8) * 512:
                                            (i // 8 + 1) * 512],
                    512, F32, queue="gpsimd", depth=4, name="pDxs", nrows=64)
    yb = p.pool.alloc_n("ybD", 4, [128, 512], F32)
    ystore = {}

    def statD(k, o):
        return w3q[k][:, o * 128:(o + 1) * 128], (w3ld, hc_done)

    def movD(k, t):
        buf, w = mvs.get(t * 32 + k)
        return buf[:], w, mvs.freed_when

    def epilD(o, t, ps, gmm):
        i = t * 8 + o
        xbuf, xw = xs3.get(i)
        yi = i % 4
        w = (gmm, cols_pred) + tuple(xw)
        if i >= 4:
            w = w + (ystore[i - 4],)
        f1 = p.emit("vector",
                    lambda e, o=o, yi=yi, ps=ps: e.tensor_scalar_mul(
                        yb[yi][:], ps, sv["s3"][:, o:o + 1]),
                    waits=w)
        f2 = p.emit("vector",
                    lambda e, yi=yi, xbuf=xbuf: e.tensor_add(
                        yb[yi][:], yb[yi][:], xbuf[:]),
                    waits=(f1,))
        xs3.freed_when(f2)
        ystore[i] = p.dma("sync", y_d[o][:, t * 512:(t + 1) * 512],
                          yb[yi][:], "y_st", waits=(f2,))
        return f2

    proj_kmajor(p, "pD", 32, 8, statD, movD, epilD)
    p.emit_wait("sync", (ystore[63],))
    p.emit_wait("gpsimd", (ystore[63],))
    p.finalize()
    return nc


def _l0ffn_inputs(xc, w1_w, w1_s, w2_w, w2_s, h1_w, h1_s, h2_w, h2_s,
                  w3_w, w3_s):
    return {
        "x": trows(np.ascontiguousarray(np.asarray(xc, np.float32).T)),
        "w1st": stat_tiles(np.asarray(w1_w, np.float32).T),
        "w2st": stat_tiles(np.asarray(w2_w, np.float32).T),
        "h1st": stat_tiles(np.asarray(h1_w, np.float32).T),
        "h2st": stat_tiles(np.asarray(h2_w, np.float32).T),
        "w3T": trows(np.ascontiguousarray(np.asarray(w3_w, np.float32).T)),
        "s1": colform(w1_s), "s2": colform(w2_s), "sh1": colform(h1_s),
        "sh2": colform(h2_s), "s3": colform(w3_s),
    }


def run_l0_ffn(x, w1_w, w1_s, w2_w, w2_s, h1_w, h1_s, h2_w, h2_s, w3_w,
               w3_s, n_cores=8):
    nc = bass.Bass("TRN2", target_bir_lowering=False, debug=False)
    build_l0_ffn(nc)
    in_maps = [_l0ffn_inputs(np.ascontiguousarray(x[i]), w1_w, w1_s, w2_w,
                             w2_s, h1_w, h1_s, h2_w, h2_s, w3_w, w3_s)
               for i in range(n_cores)]
    res = run_bass_kernel_spmd(nc, in_maps, core_ids=list(range(n_cores)))
    outs = [np.asarray(r["y"]).reshape(1024, T).T for r in res.results]
    return np.stack(outs, 0)


def ln_u(p, consts, x_d, name, KT=8):
    mub, rb, (bp1, bp2) = ln_outer(p, name, consts, lambda k: x_d[k], KT=KT)
    p.barrier()
    u = p.pool.alloc_n(name + "_u", KT, [128, T], BF16)
    xs = RowStream(p, lambda k: x_d[k], T, F32, queue="sync", depth=2,
                   name=name + "_ux", nrows=KT)
    tmt = p.pool.alloc(name + "_tm2", [128, T], F32)
    up = None
    for k in range(KT):
        buf, w = xs.get(k)
        w = tuple(w) + (bp1, bp2)
        if up is not None:
            w = w + (up,)
        e1 = p.emit("vector", lambda e, buf=buf: e.tensor_sub(
            tmt[:], buf[:], mub[:]), waits=w)
        xs.freed_when(e1)
        up = p.emit("vector", lambda e, k=k: e.tensor_mul(u[k][:], tmt[:],
                                                          rb[:]), waits=(e1,))
    xs.done()
    p.free_later(tmt, mub, rb)
    return u, up


def build_mha(nc, H, G, neg, qkv_bias):
    p = Prog(nc)
    EKV = G * 64
    x_d = nc.dram_tensor("x", [8, 128, T], F32, kind="ExternalInput").ap()
    wq_d = nc.dram_tensor("wqT", [8, 128, 1024], F32,
                          kind="ExternalInput").ap()
    wk_d = nc.dram_tensor("wkT", [8, 128, EKV], F32,
                          kind="ExternalInput").ap()
    wv_d = nc.dram_tensor("wvT", [8, 128, EKV], F32,
                          kind="ExternalInput").ap()
    wo_d = nc.dram_tensor("woT", [8, 128, 1024], F32,
                          kind="ExternalInput").ap()
    bo_in = nc.dram_tensor("bo", [128, 8], F32, kind="ExternalInput").ap()
    if qkv_bias:
        bq_in = nc.dram_tensor("bq", [128, 8], F32, kind="ExternalInput").ap()
        bk_in = nc.dram_tensor("bk", [128, G // 2], F32,
                               kind="ExternalInput").ap()
        bv_in = nc.dram_tensor("bv", [128, G // 2], F32,
                               kind="ExternalInput").ap()
    y_d = nc.dram_tensor("y", [8, 128, T], F32, kind="ExternalOutput").ap()

    qT_d = nc.dram_tensor("qT", [8, 128, T], BF16, kind="Internal").ap()
    kT_d = nc.dram_tensor("kT", [G // 2, 128, T], BF16, kind="Internal").ap()
    v_d = nc.dram_tensor("vD", [32, 128, EKV], BF16, kind="Internal").ap()
    zT_d = nc.dram_tensor("zT", [8, 128, T], BF16, kind="Internal").ap()

    consts = Consts(p)
    p.barrier()
    sv = {}
    cols_pred = None
    cols = [("bo", 8)]
    if qkv_bias:
        cols += [("bq", 8), ("bk", G // 2), ("bv", G // 2)]
    ins = {"bo": bo_in}
    if qkv_bias:
        ins.update({"bq": bq_in, "bk": bk_in, "bv": bv_in})
    for nm, w in cols:
        sv[nm] = p.pool.alloc("sv_" + nm, [128, w], F32)
        cols_pred = p.dma("sync", sv[nm][:], ins[nm][:], "cols_ld")

    u, up = ln_u(p, consts, x_d, "lnA")
    p.barrier()

    wq = p.pool.alloc_n("wq_", 8, [128, 1024], BF16)
    wk = p.pool.alloc_n("wk_", 8, [128, EKV], BF16)
    wv = p.pool.alloc_n("wv_", 8, [128, EKV], BF16)
    wld = None
    for k in range(8):
        wld = p.dma("gpsimd", wq[k][:], wq_d[k], "w_ld")
        wld = p.dma("gpsimd", wk[k][:], wk_d[k], "w_ld")
        wld = p.dma("gpsimd", wv[k][:], wv_d[k], "w_ld")

    ob = p.pool.alloc_n("obq", 2, [128, T], BF16)
    store = {}

    def emajor(tag, wrows, dst_d, OT, bias):
        for o in range(OT):
            evac = None
            for t in range(8):
                bi = (o * 8 + t) % 6
                ps = p.bank(bi)
                w0 = p.use_bank(bi) + (wld, up)
                if t == 0 and (tag, o - 2) in store:
                    w0 = w0 + (store[(tag, o - 2)],)
                mm = None
                for k in range(8):
                    mm = p.emit("tensor",
                                lambda e, ps=ps, k=k, o=o, t=t, wrows=wrows:
                                e.matmul(ps, wrows[k][:, o * 128:
                                                      (o + 1) * 128],
                                         u[k][:, t * 512:(t + 1) * 512],
                                         start=(k == 0), stop=(k == 7)),
                                waits=w0 if k == 0 else (), tinc=(k == 7))
                if bias is None:
                    evac = p.emit("vector",
                                  lambda e, ps=ps, o=o, t=t: e.tensor_copy(
                                      ob[o % 2][:, t * 512:(t + 1) * 512],
                                      ps), waits=(mm,))
                else:
                    evac = p.emit("vector",
                                  lambda e, ps=ps, o=o, t=t:
                                  e.tensor_scalar_add(
                                      ob[o % 2][:, t * 512:(t + 1) * 512],
                                      ps, bias[:, o:o + 1]),
                                  waits=(mm, cols_pred))
                p.done_bank(bi, evac)
            store[(tag, o)] = p.dma("sync", dst_d[o], ob[o % 2][:],
                                    tag + "_st", waits=(evac,))

    emajor("q", wq, qT_d, 8, sv["bq"] if qkv_bias else None)
    emajor("k", wk, kT_d, G // 2, sv["bk"] if qkv_bias else None)

    vb = p.pool.alloc_n("vb", 2, [128, EKV], BF16)
    EH = EKV // 512
    for tt in range(32):
        evac = None
        for eh in range(max(1, EH)):
            ew = min(512, EKV)
            bi = 6 + (tt * max(1, EH) + eh) % 2
            ps = p.bank(bi, 128, ew)
            w0 = p.use_bank(bi) + (wld, up)
            if ("v", tt - 2) in store and eh == 0:
                w0 = w0 + (store[("v", tt - 2)],)
            mm = None
            for k in range(8):
                mm = p.emit("tensor",
                            lambda e, ps=ps, k=k, tt=tt, eh=eh, ew=ew:
                            e.matmul(ps, u[k][:, tt * 128:(tt + 1) * 128],
                                     wv[k][:, eh * 512:eh * 512 + ew],
                                     start=(k == 0), stop=(k == 7)),
                            waits=w0 if k == 0 else (), tinc=(k == 7))
            evac = p.emit("vector",
                          lambda e, ps=ps, tt=tt, eh=eh, ew=ew:
                          e.tensor_copy(
                              vb[tt % 2][:, eh * 512:eh * 512 + ew], ps),
                          waits=(mm,))
            p.done_bank(bi, evac)
        store[("v", tt)] = p.dma("sync", v_d[tt], vb[tt % 2][:], "v_st",
                                 waits=(evac,))
    p.free_later(*wq, *wk, *wv, *ob, *vb, *u)
    p.barrier()

    KTh = p.pool.alloc("KTh", [64, T], BF16)
    QTh = p.pool.alloc("QTh", [64, T], BF16)
    Vh = p.pool.alloc("Vh", [128, 32 * 65], BF16)
    masks = p.pool.alloc_n("msk", 4, [128, 512], F32)
    pT = p.pool.alloc_n("pT", 2, [128, 512], BF16)
    tmx = p.pool.alloc_n("tmx", 2, [128, 512], F32)
    srow = p.pool.alloc("srow", [1, 512], F32)
    rbc = p.pool.alloc("rbc", [64, 512], F32)
    zb = p.pool.alloc_n("zb", 2, [64, 512], BF16)
    mp = None
    for r in range(4):
        m0 = p.emit("gpsimd", lambda e, r=r: e.memset(masks[r][:], 0.0))
        mp = p.emit("gpsimd",
                    lambda e, r=r: e.affine_select(
                        masks[r][:], masks[r][:], [[1, 512]], ALU.is_ge,
                        float(neg), base=-128 * r, channel_multiplier=-1),
                    waits=(m0,))
    onesV = p.emit("vector", lambda e: e.memset(
        Vh[:].rearrange("p (kt c) -> p kt c", c=65)[:, :, 64:65], 1.0))

    zst = {}
    hload = None
    zb_free = {}
    for h in range(16):
        g = h * G // 16
        wl = (store[("q", 7)], store[("k", G // 2 - 1)], store[("v", 31)])
        if hload is not None:
            wl = wl + (hload,)
        l1 = p.dma("sync", QTh[:], qT_d[h // 2][(h % 2) * 64:
                                                (h % 2) * 64 + 64, :],
                   "h_ld", waits=wl)
        l2 = p.dma("sync", KTh[:], kT_d[g // 2][(g % 2) * 64:
                                                (g % 2) * 64 + 64, :],
                   "h_ld")
        l3 = p.dma("sync",
                   Vh[:].rearrange("p (kt c) -> p kt c", c=65)[:, :, 0:64],
                   v_d[:, :, g * 64:(g + 1) * 64].rearrange(
                       "kt p e -> p kt e"),
                   "h_ld", waits=(onesV,))
        hl = ("h_ld", p.cnt["h_ld"])
        last_z = None
        for qc in range(8):
            ob_bank = (h * 8 + qc) % 3
            pso = p.bank(ob_bank, 65, 512)
            nkt = 4 * qc + 4
            exps = {}
            mm2 = None
            for kt in range(nkt):
                sbank = 3 + (kt % 2)
                psc = p.bank(sbank)
                w0 = p.use_bank(sbank)
                if kt == 0 and qc == 0:
                    w0 = w0 + (hl,)
                if kt == 0:
                    w0 = w0 + p.use_bank(ob_bank)
                mm1 = p.emit("tensor",
                             lambda e, psc=psc, kt=kt, qc=qc: e.matmul(
                                 psc, KTh[:, kt * 128:(kt + 1) * 128],
                                 QTh[:, qc * 512:(qc + 1) * 512],
                                 start=True, stop=True),
                             waits=w0, tinc=True)
                wexp = (mm1,)
                if kt >= 2:
                    wexp = wexp + (exps[kt - 2][1],)
                rel = kt - 4 * qc
                if 0 <= rel <= 3:
                    a0 = p.emit("vector",
                                lambda e, psc=psc, rel=rel, kt=kt:
                                e.tensor_add(tmx[kt % 2][:], psc,
                                             masks[rel][:]),
                                waits=wexp + (mp,))
                    ex = p.emit("scalar",
                                lambda e, kt=kt: e.activation(
                                    pT[kt % 2][:], tmx[kt % 2][:], AF.Exp,
                                    scale=0.125),
                                waits=(a0,))
                else:
                    ex = p.emit("scalar",
                                lambda e, psc=psc, kt=kt: e.activation(
                                    pT[kt % 2][:], psc, AF.Exp, scale=0.125),
                                waits=wexp)
                mm2 = p.emit("tensor",
                             lambda e, pso=pso, kt=kt: e.matmul(
                                 pso, Vh[:, kt * 65:(kt + 1) * 65],
                                 pT[kt % 2][:], start=(kt == 0),
                                 stop=(kt == nkt - 1)),
                             waits=(ex,), tinc=True)
                exps[kt] = (ex, mm2)
            r1 = p.emit("vector", lambda e, pso=pso: e.reciprocal(
                srow[:], pso[64:65, 0:512]), waits=(mm2,))
            bmm = p.emit("tensor",
                         lambda e: e.matmul(p.bank(5, 64, 512),
                                            consts.ones_row_f[0:1, 0:64],
                                            srow[:], start=True, stop=True),
                         waits=(r1,) + p.use_bank(5), tinc=True)
            r2 = p.emit("vector", lambda e: e.tensor_copy(
                rbc[:], p.bank(5, 64, 512)), waits=(bmm,))
            p.done_bank(5, r2)
            zi = qc % 2
            wz = (r2,)
            if zi in zb_free:
                wz = wz + (zb_free[zi],)
            z1 = p.emit("vector",
                        lambda e, pso=pso, zi=zi: e.tensor_mul(
                            tmx[zi][:64, :], pso[0:64, 0:512], rbc[:]),
                        waits=wz)
            if qkv_bias:
                z2 = p.emit("vector",
                            lambda e, zi=zi, h=h: e.tensor_scalar_add(
                                zb[zi][:], tmx[zi][0:64, :],
                                sv["bv"][(h % 2) * 64:(h % 2) * 64 + 64,
                                         h // 2:h // 2 + 1]),
                            waits=(z1, cols_pred))
            else:
                z2 = p.emit("vector",
                            lambda e, zi=zi: e.tensor_copy(
                                zb[zi][:], tmx[zi][0:64, :]),
                            waits=(z1,))
            p.done_bank(ob_bank, z2)
            zst[(h, qc)] = p.dma(
                "gpsimd",
                zT_d[h // 2][(h % 2) * 64:(h % 2) * 64 + 64,
                             qc * 512:(qc + 1) * 512],
                zb[zi][:], "z_st", waits=(z2,))
            zb_free[zi] = zst[(h, qc)]
            last_z = z2
        hload = last_z
    p.free_later(KTh, QTh, Vh, *masks, *pT, *tmx, srow, rbc, *zb)
    p.barrier()

    wo = p.pool.alloc_n("wo_", 8, [128, 1024], BF16)
    wold = None
    for k in range(8):
        wold = p.dma("gpsimd", wo[k][:], wo_d[k], "wo_ld")
    mvs = RowStream(p, lambda i: zT_d[i % 8][:, (i // 8) * 512:
                                             (i // 8 + 1) * 512],
                    512, BF16, queue="sync", depth=4, name="oPmv", nrows=64)
    xs3 = RowStream(p, lambda i: x_d[i % 8][:, (i // 8) * 512:
                                            (i // 8 + 1) * 512],
                    512, F32, queue="gpsimd", depth=4, name="oPxs", nrows=64)
    yb = p.pool.alloc_n("ybO", 4, [128, 512], F32)
    ystore = {}

    def statO(k, o):
        return wo[k][:, o * 128:(o + 1) * 128], (wold,)

    def movO(k, t):
        buf, w = mvs.get(t * 8 + k)
        return buf[:], w, mvs.freed_when

    def epilO(o, t, ps, gmm):
        i = t * 8 + o
        xbuf, xw = xs3.get(i)
        yi = i % 4
        w = (gmm, cols_pred) + tuple(xw)
        if i >= 4:
            w = w + (ystore[i - 4],)
        f1 = p.emit("vector",
                    lambda e, o=o, yi=yi, ps=ps: e.tensor_scalar_add(
                        yb[yi][:], ps, sv["bo"][:, o:o + 1]),
                    waits=w)
        f2 = p.emit("vector",
                    lambda e, yi=yi, xbuf=xbuf: e.tensor_add(
                        yb[yi][:], yb[yi][:], xbuf[:]),
                    waits=(f1,))
        xs3.freed_when(f2)
        ystore[i] = p.dma("sync", y_d[o][:, t * 512:(t + 1) * 512],
                          yb[yi][:], "y_st", waits=(f2,))
        return f2

    proj_kmajor(p, "oP", 8, 8, statO, movO, epilO)
    p.emit_wait("sync", (ystore[63],))
    p.emit_wait("gpsimd", (ystore[63],))
    p.finalize()
    return nc


def _l2attn_inputs(xc, attn_w, attn_b, out_w, out_b):
    attn_w = np.asarray(attn_w, np.float32).reshape(16, 192, 1024)
    attn_b = np.asarray(attn_b, np.float32).reshape(16, 192)
    wq = attn_w[:, 0:64].reshape(1024, 1024)
    wk = attn_w[:, 64:128].reshape(1024, 1024)
    wv = attn_w[:, 128:192].reshape(1024, 1024)
    return {
        "x": trows(np.ascontiguousarray(np.asarray(xc, np.float32).T)),
        "wqT": trows(np.ascontiguousarray(wq.T)),
        "wkT": trows(np.ascontiguousarray(wk.T)),
        "wvT": trows(np.ascontiguousarray(wv.T)),
        "woT": trows(np.ascontiguousarray(np.asarray(out_w, np.float32).T)),
        "bq": colform(attn_b[:, 0:64].reshape(-1)),
        "bk": colform(attn_b[:, 64:128].reshape(-1)),
        "bv": colform(attn_b[:, 128:192].reshape(-1)),
        "bo": colform(out_b),
    }


def run_l2_attn(x, attn_w, attn_b, out_w, out_b, n_cores=8):
    nc = bass.Bass("TRN2", target_bir_lowering=False, debug=False)
    build_mha(nc, 16, 16, -1e4, True)
    in_maps = [_l2attn_inputs(np.ascontiguousarray(x[i]), attn_w, attn_b,
                              out_w, out_b) for i in range(n_cores)]
    res = run_bass_kernel_spmd(nc, in_maps, core_ids=list(range(n_cores)))
    outs = [np.asarray(r["y"]).reshape(1024, T).T for r in res.results]
    return np.stack(outs, 0)


def _l1attn_inputs(xc, W_Q, W_K, W_V, W_out, b_out):
    wq = np.asarray(W_Q, np.float32).reshape(1024, 1024)
    wk = np.asarray(W_K, np.float32).reshape(1024, 512)
    wv = np.asarray(W_V, np.float32).reshape(1024, 512)
    wo = np.asarray(W_out, np.float32).transpose(1, 0, 2).reshape(1024, 1024)
    return {
        "x": trows(np.ascontiguousarray(np.asarray(xc, np.float32).T)),
        "wqT": trows(np.ascontiguousarray(wq)),
        "wkT": trows(np.ascontiguousarray(wk)),
        "wvT": trows(np.ascontiguousarray(wv)),
        "woT": trows(np.ascontiguousarray(wo)),
        "bo": colform(b_out),
    }


def run_l1_attn(x, W_Q, W_K, W_V, W_out, b_out, n_cores=8):
    nc = bass.Bass("TRN2", target_bir_lowering=False, debug=False)
    build_mha(nc, 16, 8, -1e6, False)
    in_maps = [_l1attn_inputs(np.ascontiguousarray(x[i]), W_Q, W_K, W_V,
                              W_out, b_out) for i in range(n_cores)]
    res = run_bass_kernel_spmd(nc, in_maps, core_ids=list(range(n_cores)))
    outs = [np.asarray(r["y"]).reshape(1024, T).T for r in res.results]
    return np.stack(outs, 0)


def build_moe(nc, renorm, gelu_tanh, cap, has_bg):
    p = Prog(nc)
    E = 8
    x_d = nc.dram_tensor("x", [8, 128, T], F32, kind="ExternalInput").ap()
    wg_d = nc.dram_tensor("wgT", [8, 128, E], F32, kind="ExternalInput").ap()
    w1_d = nc.dram_tensor("w1st", [E, 32, 8, 128, 128], F32,
                          kind="ExternalInput").ap()
    w2_d = nc.dram_tensor("w2st", [E, 8, 32, 128, 128], F32,
                          kind="ExternalInput").ap()
    b1_d = nc.dram_tensor("b1c", [E, 128, 32], F32, kind="ExternalInput").ap()
    b2_d = nc.dram_tensor("b2st", [E, 1024], F32, kind="ExternalInput").ap()
    selp_d = nc.dram_tensor("selp", [8, 1], F32, kind="ExternalInput").ap()
    if has_bg:
        bg_d = nc.dram_tensor("bg", [1, E], F32, kind="ExternalInput").ap()
    y_d = nc.dram_tensor("y", [8, 128, T], F32, kind="ExternalOutput").ap()

    u_dram = nc.dram_tensor("uD", [8, 128, T], BF16, kind="Internal").ap()
    gT_dram = nc.dram_tensor("gTD", [8, T], F32, kind="Internal").ap()
    tot_d = nc.dram_tensor("totD", [1, 2 * E], F32, kind="Internal").ap()
    ag_d = nc.dram_tensor("agD", [8, 2 * E], F32, kind="Internal").ap()

    consts = Consts(p)
    gelu_f = AF.Gelu_apprx_tanh if gelu_tanh else AF.Gelu
    p.barrier()
    wg = p.pool.alloc("wg", [128, 8 * E], BF16)
    wgv = wg[:].rearrange("p (k e) -> p k e", e=E)
    wgld = p.dma("gpsimd", wgv, wg_d.rearrange("k p e -> p k e"), "wg_ld")
    selp = p.pool.alloc("selp", [8, 1], F32)
    ones8 = p.pool.alloc("ones8", [8, 1], F32)
    sp1 = p.dma("sync", selp[:], selp_d[:], "sel_ld")
    sp2 = p.emit("vector", lambda e: e.memset(ones8[:], 1.0))
    bgb = p.pool.alloc("bgb", [128, E], F32)
    if has_bg:
        bgr = p.pool.alloc("bgr", [1, E], F32)
        bgp = p.dma("sync", bgr[:], bg_d[:], "sel_ld")
        bmm = p.emit("tensor", lambda e: e.matmul(
            p.bank(6, 128, E), consts.ones_row_f[:], bgr[:],
            start=True, stop=True), waits=(bgp,), tinc=True)
        bgc = p.emit("vector", lambda e: e.tensor_copy(bgb[:],
                                                       p.bank(6, 128, E)),
                     waits=(bmm,))
        p.done_bank(6, bgc)
    else:
        bgc = p.emit("vector", lambda e: e.memset(bgb[:], 0.0))

    u, up = ln_u(p, consts, x_d, "lnM")
    ust = None
    for k in range(8):
        ust = p.dma("sync", u_dram[k], u[k][:], "u_st", waits=(up,))

    trilX = p.pool.alloc("trilX", [128, 128], BF16)
    t0 = p.emit("gpsimd", lambda e: e.memset(trilX[:], 1.0))
    t1 = p.emit("gpsimd", lambda e: e.affine_select(
        trilX[:], trilX[:], [[1, 128]], ALU.is_ge, 0.0, base=-1,
        channel_multiplier=-1), waits=(t0,))
    identf = p.pool.alloc("identf", [128, 128], F32)
    i0 = p.emit("gpsimd", lambda e: e.memset(identf[:], 1.0))
    i1 = p.emit("gpsimd", lambda e: e.affine_select(
        identf[:], identf[:], [[1, 128]], ALU.is_equal, 0.0,
        channel_multiplier=-1), waits=(i0,))
    oh1 = p.pool.alloc("oh1", [128, 32 * E], BF16)
    oh2 = p.pool.alloc("oh2", [128, 32 * E], BF16)
    gv1 = p.pool.alloc("gv1", [128, 32], F32)
    gv2 = p.pool.alloc("gv2", [128, 32], F32)
    ex1 = p.pool.alloc("ex1", [128, 32 * E], F32)
    ex2 = p.pool.alloc("ex2", [128, 32 * E], F32)
    carry = p.pool.alloc("carry", [1, 2 * E], F32)
    pr = p.pool.alloc("pr", [128, E], F32)
    sc1 = p.pool.alloc("sc1", [128, E], F32)
    sc2 = p.pool.alloc("sc2", [128, E], F32)
    mcol = p.pool.alloc("mcol", [128, 1], F32)
    ccz = p.emit("vector", lambda e: e.memset(carry[:], 0.0))
    cprev = [ccz, ccz]
    for tt in range(32):
        mm = None
        for k in range(8):
            w = (wgld, up) + (p.use_bank(5) if k == 0 else ())
            if k == 0 and tt == 0:
                w = w + (bgc,)
            mm = p.emit("tensor",
                        lambda e, k=k, tt=tt: e.matmul(
                            p.bank(5, 128, E),
                            u[k][:, tt * 128:(tt + 1) * 128],
                            wgv[:, k, :], start=(k == 0), stop=(k == 7)),
                        waits=w if k == 0 else (), tinc=(k == 7))
        c1 = p.emit("vector", lambda e: e.tensor_add(pr[:], p.bank(5, 128, E),
                                                     bgb[:]), waits=(mm,))
        p.done_bank(5, c1)
        c2 = p.emit("vector", lambda e: e.reduce_max(mcol[:], pr[:], AX.X,
                                                     negate=True),
                    waits=(c1,))
        c3 = p.emit("scalar", lambda e: e.activation(pr[:], pr[:], AF.Exp,
                                                     bias=mcol[:]),
                    waits=(c2,))
        c4 = p.emit("vector", lambda e: e.reduce_sum(mcol[:], pr[:], AX.X),
                    waits=(c3,))
        c5 = p.emit("vector", lambda e: e.reciprocal(mcol[:], mcol[:]),
                    waits=(c4,))
        c6 = p.emit("vector", lambda e: e.tensor_scalar_mul(pr[:], pr[:],
                                                            mcol[:]),
                    waits=(c5,))
        d1 = p.emit("vector", lambda e: e.reduce_max(mcol[:], pr[:], AX.X),
                    waits=(c6,))
        d2 = p.emit("vector", lambda e, tt=tt: e.tensor_copy(
            gv1[:, tt:tt + 1], mcol[:]), waits=(d1,))
        d3 = p.emit("vector", lambda e: e.tensor_scalar(
            sc1[:], pr[:], mcol[:], None, ALU.is_ge), waits=(d2,))
        d4 = p.emit("vector", lambda e: e.tensor_tensor_scan(
            sc2[:], sc1[:], sc1[:], 0.0, ALU.add, ALU.bypass), waits=(d3,))
        d5 = p.emit("vector", lambda e: e.tensor_single_scalar(
            sc2[:], sc2[:], 1.0, ALU.is_equal), waits=(d4,))
        d6 = p.emit("vector", lambda e: e.tensor_mul(sc1[:], sc1[:],
                                                     sc2[:]), waits=(d5,))
        d7 = p.emit("vector", lambda e, tt=tt: e.tensor_copy(
            oh1[:, tt * E:(tt + 1) * E], sc1[:]), waits=(d6,))
        e1 = p.emit("vector", lambda e: e.scalar_tensor_tensor(
            pr[:], sc1[:], -1e6, pr[:], ALU.mult, ALU.add), waits=(d7,))
        e2 = p.emit("vector", lambda e: e.reduce_max(mcol[:], pr[:], AX.X),
                    waits=(e1,))
        e3 = p.emit("vector", lambda e, tt=tt: e.tensor_copy(
            gv2[:, tt:tt + 1], mcol[:]), waits=(e2,))
        e4 = p.emit("vector", lambda e: e.tensor_scalar(
            sc1[:], pr[:], mcol[:], None, ALU.is_ge), waits=(e3,))
        e5 = p.emit("vector", lambda e: e.tensor_tensor_scan(
            sc2[:], sc1[:], sc1[:], 0.0, ALU.add, ALU.bypass), waits=(e4,))
        e6 = p.emit("vector", lambda e: e.tensor_single_scalar(
            sc2[:], sc2[:], 1.0, ALU.is_equal), waits=(e5,))
        e7 = p.emit("vector", lambda e: e.tensor_mul(sc1[:], sc1[:],
                                                     sc2[:]), waits=(e6,))
        e8 = p.emit("vector", lambda e, tt=tt: e.tensor_copy(
            oh2[:, tt * E:(tt + 1) * E], sc1[:]), waits=(e7,))
        for j, (oh, ex, ohp) in enumerate(((oh1, ex1, d7), (oh2, ex2, e8))):
            bi = 6 + j
            p.emit("tensor",
                   lambda e, oh=oh, tt=tt, bi=bi: e.matmul(
                       p.bank(bi, 128, E), trilX[:],
                       oh[:, tt * E:(tt + 1) * E],
                       start=True, stop=False),
                   waits=(t1, ohp, cprev[j]) + p.use_bank(bi))
            pm2 = p.emit("tensor",
                         lambda e, j=j, bi=bi: e.matmul(
                             p.bank(bi, 128, E), consts.ones_row_f[:],
                             carry[:, j * E:(j + 1) * E],
                             start=False, stop=True),
                         tinc=True)
            cp = p.emit("vector",
                        lambda e, ex=ex, tt=tt, bi=bi: e.tensor_copy(
                            ex[:, tt * E:(tt + 1) * E], p.bank(bi, 128, E)),
                        waits=(pm2,))
            tsm = p.emit("tensor",
                         lambda e, oh=oh, tt=tt, bi=bi: e.matmul(
                             p.bank(bi, 1, E), consts.ones_col_bf[:],
                             oh[:, tt * E:(tt + 1) * E],
                             start=True, stop=True),
                         waits=(cp,), tinc=True)
            cprev[j] = p.emit("vector",
                              lambda e, j=j, bi=bi: e.tensor_add(
                                  carry[:, j * E:(j + 1) * E],
                                  carry[:, j * E:(j + 1) * E],
                                  p.bank(bi, 1, E)),
                              waits=(tsm,))
            p.done_bank(bi, cprev[j])
    tstore = p.dma("sync", tot_d[:], carry[:], "tot_st",
                   waits=(cprev[0], cprev[1]))
    cc = p.emit("gpsimd",
                lambda e: e.collective_compute(
                    "AllGather", ALU.bypass,
                    replica_groups=[[0, 1, 2, 3, 4, 5, 6, 7]],
                    ins=[tot_d[:]], outs=[ag_d[:]]),
                waits=(tstore,))
    ag = p.pool.alloc("ag", [8, 2 * E], F32)
    agp = p.dma("sync", ag[:], ag_d[:], "ag_ld", waits=(cc,))
    base = p.pool.alloc("base", [1, 2 * E], F32)
    baseb = p.pool.alloc_n("baseb", 2, [128, E], F32)
    bb = []
    for j in range(2):
        if j == 0:
            m = p.emit("tensor", lambda e: e.matmul(
                p.bank(6, 1, E), selp[:], ag[:, 0:E], start=True, stop=True),
                waits=(agp, sp1, sp2) + p.use_bank(6), tinc=True)
        else:
            p.emit("tensor", lambda e: e.matmul(
                p.bank(7, 1, E), ones8[:], ag[:, 0:E], start=True,
                stop=False),
                waits=(agp, sp1, sp2) + p.use_bank(7))
            m = p.emit("tensor", lambda e: e.matmul(
                p.bank(7, 1, E), selp[:], ag[:, E:2 * E], start=False,
                stop=True), tinc=True)
        c = p.emit("vector", lambda e, j=j: e.tensor_copy(
            base[:, j * E:(j + 1) * E], p.bank(6 + j, 1, E)), waits=(m,))
        m2 = p.emit("tensor", lambda e, j=j: e.matmul(
            p.bank(6 + j, 128, E), consts.ones_row_f[:],
            base[:, j * E:(j + 1) * E],
            start=True, stop=True), waits=(c,), tinc=True)
        c2 = p.emit("vector", lambda e, j=j: e.tensor_copy(
            baseb[j][:], p.bank(6 + j, 128, E)), waits=(m2,))
        p.done_bank(6 + j, c2)
        bb.append(c2)
    gTk = p.pool.alloc("gTk", [E, T], F32)
    w8 = p.pool.alloc("w8", [128, E], F32)
    gsum = None
    for tt in range(32):
        k1 = p.emit("vector", lambda e, tt=tt: e.tensor_add(
            sc1[:], ex1[:, tt * E:(tt + 1) * E], baseb[0][:]),
            waits=(bb[0], bb[1]) if tt == 0 else (gsum,))
        k2 = p.emit("vector", lambda e: e.tensor_single_scalar(
            sc1[:], sc1[:], float(cap), ALU.is_lt), waits=(k1,))
        k3 = p.emit("vector", lambda e, tt=tt: e.tensor_mul(
            sc1[:], sc1[:], oh1[:, tt * E:(tt + 1) * E]), waits=(k2,))
        k4 = p.emit("vector", lambda e, tt=tt: e.tensor_add(
            sc2[:], ex2[:, tt * E:(tt + 1) * E], baseb[1][:]), waits=(k3,))
        k5 = p.emit("vector", lambda e: e.tensor_single_scalar(
            sc2[:], sc2[:], float(cap), ALU.is_lt), waits=(k4,))
        k6 = p.emit("vector", lambda e, tt=tt: e.tensor_mul(
            sc2[:], sc2[:], oh2[:, tt * E:(tt + 1) * E]), waits=(k5,))
        if renorm:
            r1 = p.emit("vector", lambda e, tt=tt: e.tensor_add(
                mcol[:], gv1[:, tt:tt + 1], gv2[:, tt:tt + 1]), waits=(k6,))
            r2 = p.emit("vector", lambda e: e.reciprocal(mcol[:], mcol[:]),
                        waits=(r1,))
            g1 = p.emit("vector", lambda e, tt=tt: e.tensor_scalar(
                sc1[:], sc1[:], gv1[:, tt:tt + 1], mcol[:], ALU.mult,
                ALU.mult), waits=(r2,))
            g2 = p.emit("vector", lambda e, tt=tt: e.tensor_scalar(
                sc2[:], sc2[:], gv2[:, tt:tt + 1], mcol[:], ALU.mult,
                ALU.mult), waits=(g1,))
        else:
            g1 = p.emit("vector", lambda e, tt=tt: e.tensor_scalar_mul(
                sc1[:], sc1[:], gv1[:, tt:tt + 1]), waits=(k6,))
            g2 = p.emit("vector", lambda e, tt=tt: e.tensor_scalar_mul(
                sc2[:], sc2[:], gv2[:, tt:tt + 1]), waits=(g1,))
        g3 = p.emit("vector", lambda e: e.tensor_add(w8[:], sc1[:], sc2[:]),
                    waits=(g2,))
        tm = p.emit("tensor", lambda e: e.matmul(
            p.bank(6, E, 128), w8[:], identf[:], start=True, stop=True),
            waits=(g3, i1) + p.use_bank(6), tinc=True)
        gsum = p.emit("vector", lambda e, tt=tt: e.tensor_copy(
            gTk[:, tt * 128:(tt + 1) * 128], p.bank(6, E, 128)),
            waits=(tm,))
        p.done_bank(6, gsum)
    gst = p.dma("sync", gT_dram[:], gTk[:], "gT_st", waits=(gsum,))
    p.free_later(oh1, oh2, ex1, ex2, sc1, sc2, pr, mcol, gv1, gv2, trilX,
                 identf, w8, ag, base, *baseb, carry, wg, selp, ones8, bgb,
                 *u)
    p.barrier()

    b2s = p.pool.alloc("b2s", [E, 1024], F32)
    b2p = p.dma("sync", b2s[:], b2_d[:], "b2_ld")
    xsI = RowStream(p, lambda i: x_d[i % 8][:, (i // 8) * 512:
                                            (i // 8 + 1) * 512],
                    512, F32, queue="gpsimd", depth=4, name="xsI", nrows=64)
    ybI = p.pool.alloc_n("ybI", 4, [128, 512], F32)
    yst = {}
    for t in range(8):
        for o in range(8):
            i = t * 8 + o
            bi = i % 6
            mm = p.emit("tensor", lambda e, o=o, t=t, bi=bi: e.matmul(
                p.bank(bi), b2s[:, o * 128:(o + 1) * 128],
                gTk[:, t * 512:(t + 1) * 512],
                start=True, stop=True),
                waits=(b2p,) + p.use_bank(bi), tinc=True)
            xbuf, xw = xsI.get(i)
            yi = i % 4
            w = (mm,) + tuple(xw)
            if i >= 4:
                w = w + (yst[i - 4],)
            f1 = p.emit("vector", lambda e, yi=yi, bi=bi, xbuf=xbuf:
                        e.tensor_add(ybI[yi][:], p.bank(bi), xbuf[:]),
                        waits=w)
            xsI.freed_when(f1)
            p.done_bank(bi, f1)
            yst[i] = p.dma("sync", y_d[o][:, t * 512:(t + 1) * 512],
                           ybI[yi][:], "y_st", waits=(f1,))
    xsI.done()
    p.free_later(*ybI, b2s)
    p.barrier()

    b1c = p.pool.alloc("b1c", [128, 32], F32)
    wbt = p.pool.alloc("wbt", [128, T], F32)
    grow = p.pool.alloc("grow", [1, T], F32)
    hrows = p.pool.alloc_n("hr", 32, [128, 1024], BF16)
    tfg = p.pool.alloc_n("tfg", 2, [128, 512], F32)
    acb = p.pool.alloc_n("acb", 4, [128, 512], F32)
    acc_pred = {}
    b1prev = None
    for e_i in range(E):
        b1p = p.dma("sync", b1c[:], b1_d[e_i], "b1_ld",
                    waits=(b1prev,) if b1prev else ())
        gwp = p.dma("sync", grow[:], gT_dram[e_i], "gr_ld")
        wb = None
        for t in range(8):
            m = p.emit("tensor", lambda e, t=t: e.matmul(
                p.bank(7), consts.ones_row_f[:],
                grow[:, t * 512:(t + 1) * 512],
                start=True, stop=True),
                waits=(gwp,) + p.use_bank(7), tinc=True)
            wb = p.emit("vector", lambda e, t=t: e.tensor_copy(
                wbt[:, t * 512:(t + 1) * 512], p.bank(7)), waits=(m,))
            p.done_bank(7, wb)
        for tq in range(4):
            uh = RowStream(p, lambda k, tq=tq: u_dram[k][:, tq * 1024:
                                                         (tq + 1) * 1024],
                           1024, BF16, queue="sync", depth=8,
                           name=f"uh{e_i}_{tq}", nrows=8, sem="uh_ld")
            st1 = StatStream(p, w1_d[e_i], 8, 32, f"s1_{e_i}_{tq}",
                             sem="s1_ld")
            hlast = None
            tf_free = {}
            for o in range(32):
                lastmm = None
                for t in range(2):
                    it = o * 2 + t
                    bi = it % 6
                    ps = p.bank(bi)
                    mm = None
                    for k in range(8):
                        ub, uw = uh.get(k, ahead=7)
                        sap, sw = st1.get(o, k)
                        w = p.use_bank(bi) if k == 0 else ()
                        if k == 0:
                            w = w + tuple(sw)
                        if o == 0 and t == 0:
                            w = w + tuple(uw)
                        mm = p.emit("tensor",
                                    lambda e, ps=ps, sap=sap, ub=ub, k=k,
                                    t=t: e.matmul(
                                        ps, sap,
                                        ub[:, t * 512:(t + 1) * 512],
                                        start=(k == 0), stop=(k == 7)),
                                    waits=w, tinc=(k == 7))
                    lastmm = mm
                    ti = it % 2
                    wtf = (mm, b1p, wb)
                    if ti in tf_free:
                        wtf = wtf + (tf_free[ti],)
                    a1 = p.emit("scalar",
                                lambda e, ps=ps, o=o, ti=ti: e.activation(
                                    tfg[ti][:], ps, gelu_f,
                                    bias=b1c[:, o:o + 1]),
                                waits=wtf)
                    hv = p.emit("vector",
                                lambda e, o=o, t=t, tq=tq, ti=ti:
                                e.tensor_mul(
                                    hrows[o][:, t * 512:(t + 1) * 512],
                                    tfg[ti][:],
                                    wbt[:, tq * 1024 + t * 512:
                                        tq * 1024 + (t + 1) * 512]),
                                waits=(a1,))
                    tf_free[ti] = hv
                    hlast = hv
                    p.done_bank(bi, hv)
                st1.freed_when(lastmm)
            st1.done()
            uh.done()
            st2 = StatStream(p, w2_d[e_i], 32, 8, f"s2_{e_i}_{tq}",
                             sem="s2_ld")
            for oo in range(8):
                lastmm = None
                for t in range(2):
                    bi = 6 + (oo * 2 + t) % 2
                    ps = p.bank(bi)
                    mm = None
                    for k in range(32):
                        sap, sw = st2.get(oo, k)
                        w = p.use_bank(bi) if k == 0 else ()
                        if k == 0:
                            w = w + tuple(sw) + (hlast,)
                        mm = p.emit("tensor",
                                    lambda e, ps=ps, sap=sap, k=k, t=t:
                                    e.matmul(
                                        ps, sap,
                                        hrows[k][:, t * 512:(t + 1) * 512],
                                        start=(k == 0), stop=(k == 31)),
                                    waits=w, tinc=(k == 31))
                    lastmm = mm
                    ai = (oo * 2 + t) % 4
                    w = (mm,)
                    if ai in acc_pred:
                        w = w + (acc_pred[ai],)
                    f1 = p.emit("vector", lambda e, ai=ai, ps=ps:
                                e.tensor_copy(acb[ai][:], ps), waits=w)
                    p.done_bank(bi, f1)
                    acc_pred[ai] = p.dma(
                        "gpsimd",
                        y_d[oo][:, tq * 1024 + t * 512:
                                tq * 1024 + (t + 1) * 512],
                        acb[ai][:], "acc_st", waits=(f1,), accum=True)
                st2.freed_when(lastmm)
            st2.done()
            p.barrier()
        b1prev = None
    p.emit_wait("gpsimd", (("acc_st", p.cnt["acc_st"]),))
    p.emit_wait("sync", (("acc_st", p.cnt["acc_st"]),))
    p.finalize()
    return nc


def _moe_inputs(xc, Wg_dx, W1, b1, W2, b2, bg, core):
    E = 8
    selp = np.zeros((8, 1), np.float32)
    selp[:core, 0] = 1.0
    W1 = np.asarray(W1, np.float32)
    W2 = np.asarray(W2, np.float32)
    d = {
        "x": trows(np.ascontiguousarray(np.asarray(xc, np.float32).T)),
        "wgT": trows(np.ascontiguousarray(np.asarray(Wg_dx, np.float32))),
        "w1st": np.stack([stat_tiles(W1[e]) for e in range(E)]),
        "w2st": np.stack([stat_tiles(W2[e]) for e in range(E)]),
        "b1c": np.stack([colform(np.asarray(b1, np.float32)[e])
                         for e in range(E)]),
        "b2st": np.ascontiguousarray(np.asarray(b2, np.float32)),
        "selp": selp,
    }
    if bg is not None:
        d["bg"] = np.asarray(bg, np.float32).reshape(1, E)
    return d


def run_moe(x, Wg_dx, W1, b1, W2, b2, bg, renorm, gelu_tanh, cap,
            n_cores=8):
    nc = bass.Bass("TRN2", target_bir_lowering=False, debug=False)
    build_moe(nc, renorm, gelu_tanh, cap, bg is not None)
    in_maps = [_moe_inputs(np.ascontiguousarray(x[i]), Wg_dx, W1, b1, W2,
                           b2, bg, i) for i in range(n_cores)]
    res = run_bass_kernel_spmd(nc, in_maps, core_ids=list(range(n_cores)))
    outs = [np.asarray(r["y"]).reshape(1024, T).T for r in res.results]
    return np.stack(outs, 0)


def build_l0_attn(nc, debug=False):
    p = Prog(nc)
    x_d = nc.dram_tensor("x", [8, 128, T], F32, kind="ExternalInput").ap()
    wq_d = nc.dram_tensor("wqT", [8, 128, 1024], F32,
                          kind="ExternalInput").ap()
    wk_d = nc.dram_tensor("wkT", [8, 128, 1024], F32,
                          kind="ExternalInput").ap()
    wv_d = nc.dram_tensor("wvT", [8, 128, 1024], F32,
                          kind="ExternalInput").ap()
    wgp_d = nc.dram_tensor("wgpT", [8, 128, 1024], F32,
                           kind="ExternalInput").ap()
    wo_d = nc.dram_tensor("woT", [8, 128, 1024], F32,
                          kind="ExternalInput").ap()
    w1_d = nc.dram_tensor("w1T", [8, 128, 4096], F32,
                          kind="ExternalInput").ap()
    w2_d = nc.dram_tensor("w2T", [32, 128, 1024], F32,
                          kind="ExternalInput").ap()
    cols_in = {}
    for nm, w in (("bg", 8), ("ln1w", 8), ("ln1b", 8), ("ln2w", 8),
                  ("ln2b", 8), ("b1", 32), ("b2", 8)):
        cols_in[nm] = nc.dram_tensor(nm, [128, w], F32,
                                     kind="ExternalInput").ap()
    bones_d = nc.dram_tensor("bones", [128, 2], F32,
                             kind="ExternalInput").ap()
    bo2_d = nc.dram_tensor("bo2", [2, 128], F32, kind="ExternalInput").ap()
    maskT_d = nc.dram_tensor("maskT", [128, 64], F32,
                             kind="ExternalInput").ap()
    blkm_d = nc.dram_tensor("blkm", [128, 128], F32,
                            kind="ExternalInput").ap()
    y_d = nc.dram_tensor("y", [8, 128, T], F32, kind="ExternalOutput").ap()

    yd = nc.dram_tensor("yD", [8, 128, T], F32, kind="Internal").ap()
    qT_d = nc.dram_tensor("qTD", [8, 128, T], BF16, kind="Internal").ap()
    kT_d = nc.dram_tensor("kTD", [8, 128, T], BF16, kind="Internal").ap()
    vT_d = nc.dram_tensor("vTD", [8, 128, T], BF16, kind="Internal").ap()
    la_d = nc.dram_tensor("laD", [8, 128, T], F32, kind="Internal").ap()
    oT_d = nc.dram_tensor("oTD", [8, 128, T], BF16, kind="Internal").ap()
    xi_d = nc.dram_tensor("xiD", [8, 128, T], F32, kind="Internal").ap()
    h_d = nc.dram_tensor("hD", [32, 128, T], BF16, kind="Internal").ap()

    consts = Consts(p)
    p.barrier()
    sv = {}
    cols_pred = None
    for nm, w in (("bg", 8), ("ln1w", 8), ("ln1b", 8), ("ln2w", 8),
                  ("ln2b", 8), ("b1", 32), ("b2", 8)):
        sv[nm] = p.pool.alloc("sv_" + nm, [128, w], F32)
        cols_pred = p.dma("sync", sv[nm][:], cols_in[nm][:], "cols_ld")
    bones = p.pool.alloc("bones", [128, 2], BF16)
    bonesf = p.pool.alloc("bonesf", [128, 2], F32)
    bo2 = p.pool.alloc("bo2", [2, 128], F32)
    maskT = p.pool.alloc("maskT", [128, 64], BF16)
    blkm = p.pool.alloc("blkm", [128, 128], F32)
    identb = p.pool.alloc("identb", [128, 128], BF16)
    cols_pred = p.dma("gpsimd", bones[:], bones_d[:], "cst_ld")
    cols_pred = p.dma("sync", bonesf[:], bones_d[:], "cst_ld2")
    cols_pred = p.dma("sync", bo2[:], bo2_d[:], "cst_ld2")
    cols_pred = p.dma("gpsimd", maskT[:], maskT_d[:], "cst_ld")
    cols_pred = p.dma("sync", blkm[:], blkm_d[:], "cst_ld2")
    ib0 = p.emit("gpsimd", lambda e: e.memset(identb[:], 1.0))
    ib1 = p.emit("gpsimd", lambda e: e.affine_select(
        identb[:], identb[:], [[1, 128]], ALU.is_equal, 0.0,
        channel_multiplier=-1), waits=(ib0,))
    cst_w = (("cst_ld", p.cnt["cst_ld"]), ("cst_ld2", p.cnt["cst_ld2"]), ib1)

    rows = p.pool.alloc_n("r", 4, [1, T], F32)
    a1, a2, mu0, r0 = rows
    tmr = p.pool.alloc("tmr", [1, T], F32)
    var = p.pool.alloc("var", [1, T], F32)
    xs = RowStream(p, lambda k: x_d[k], T, F32, queue="sync", depth=2,
                   name="xs0", nrows=8)
    sp = ln_stats_stream(p, "st0", consts, 8, lambda k: xs.get(k), a1, a2,
                         stream=xs)
    xs.done()
    lp = ln_rows(p, "lr0", a1, a2, D, (sp,), mu0, r0, var, tmr)
    s01 = a1
    d1 = p.emit("vector", lambda e: e.tensor_mul(tmr[:], r0[:], r0[:]),
                waits=(lp,))
    d2 = p.emit("vector", lambda e: e.tensor_mul(tmr[:], tmr[:], var[:]),
                waits=(d1,))
    d3 = p.emit("scalar", lambda e: e.activation(tmr[:], tmr[:], AF.Ln,
                                                 bias=LN_EPS), waits=(d2,))
    d4 = p.emit("scalar", lambda e: e.activation(tmr[:], tmr[:], AF.Exp,
                                                 scale=-0.5), waits=(d3,))
    d5 = p.emit("vector", lambda e: e.tensor_mul(s01[:], r0[:], tmr[:]),
                waits=(d4,))
    p.free_later(tmr, var)
    p.barrier()
    mub = p.pool.alloc("mub", [128, T], F32)
    r0b = p.pool.alloc("r0b", [128, T], F32)
    s01b = p.pool.alloc("s01b", [128, T], F32)
    bc1 = bcast_row(p, "bcA", mu0, mub, consts, (lp,))
    bc2 = bcast_row(p, "bcB", r0, r0b, consts, (lp, d1))
    bc3 = bcast_row(p, "bcC", s01, s01b, consts, (d5,))
    p.free_later(*rows)
    p.barrier()

    xn = p.pool.alloc_n("xn_", 8, [128, T], BF16)
    xs2 = RowStream(p, lambda k: x_d[k], T, F32, queue="sync", depth=2,
                    name="xs2", nrows=8)
    tmt = p.pool.alloc("tmt", [128, T], F32)
    xnp = None
    for k in range(8):
        buf, w = xs2.get(k)
        w = tuple(w) + (bc1, bc2, bc3, cols_pred)
        if xnp is not None:
            w = w + (xnp,)
        e1 = p.emit("vector", lambda e, buf=buf: e.tensor_sub(
            tmt[:], buf[:], mub[:]), waits=w)
        e2 = p.emit("vector", lambda e, buf=buf: e.tensor_mul(
            buf[:], tmt[:], r0b[:]), waits=(e1,))
        yst = p.dma("gpsimd", yd[k], buf[:], "y0_st", waits=(e2,))
        xs2.freed_when(yst)
        e3 = p.emit("vector", lambda e: e.tensor_mul(tmt[:], tmt[:],
                                                     s01b[:]), waits=(e2,))
        xnp = p.emit("vector", lambda e, k=k: e.tensor_scalar(
            xn[k][:], tmt[:], sv["ln1w"][:, k:k + 1], sv["ln1b"][:, k:k + 1],
            ALU.mult, ALU.add), waits=(e3,))
    xs2.done()
    p.free_later(tmt, mub, r0b, s01b)
    p.barrier()

    wq = p.pool.alloc_n("wq_", 8, [128, 1024], BF16)
    wk = p.pool.alloc_n("wk_", 8, [128, 1024], BF16)
    wv = p.pool.alloc_n("wv_", 8, [128, 1024], BF16)
    wgp = p.pool.alloc_n("wgp_", 8, [128, 1024], BF16)
    wld = None
    for k in range(8):
        wld = p.dma("gpsimd", wq[k][:], wq_d[k], "w_ld")
        wld = p.dma("gpsimd", wk[k][:], wk_d[k], "w_ld")
        wld = p.dma("gpsimd", wv[k][:], wv_d[k], "w_ld")
        wld = p.dma("gpsimd", wgp[k][:], wgp_d[k], "w_ld")

    qraw = p.pool.alloc("qraw", [128, 512], F32)
    qsq = p.pool.alloc("qsq", [128, 512], BF16)
    nrm = p.pool.alloc("nrm", [2, 512], F32)
    nrb = p.pool.alloc("nrb", [128, 512], F32)
    ob = p.pool.alloc_n("obp", 2, [128, T], BF16)
    store = {}
    ob_guard = {}
    qsq_guard = [None]

    def qk_norm_proj(tag, wrows, dst_d):
        for o in range(8):
            evac = None
            for t in range(8):
                bi = (o * 8 + t) % 3
                ps = p.bank(bi)
                w0 = p.use_bank(bi) + (wld, xnp)
                if t == 0 and (o % 2) in ob_guard:
                    w0 = w0 + (ob_guard[o % 2],)
                mm = None
                for k in range(8):
                    mm = p.emit("tensor",
                                lambda e, ps=ps, k=k, o=o, t=t,
                                wrows=wrows: e.matmul(
                                    ps, wrows[k][:, o * 128:(o + 1) * 128],
                                    xn[k][:, t * 512:(t + 1) * 512],
                                    start=(k == 0), stop=(k == 7)),
                                waits=w0 if k == 0 else (), tinc=(k == 7))
                v1 = p.emit("vector", lambda e, ps=ps: e.tensor_copy(
                    qraw[:], ps), waits=(mm, evac) if evac else (mm,))
                p.done_bank(bi, v1)
                wv2 = (v1,)
                if qsq_guard[0] is not None:
                    wv2 = wv2 + (qsq_guard[0],)
                v2 = p.emit("scalar", lambda e: e.activation(
                    qsq[:], qraw[:], AF.Square), waits=wv2)
                m2 = p.emit("tensor", lambda e: e.matmul(
                    p.bank(3, 2, 512), bones[:], qsq[:], start=True,
                    stop=True), waits=(v2,) + p.use_bank(3) + cst_w,
                    tinc=True)
                qsq_guard[0] = m2
                v3 = p.emit("vector", lambda e: e.tensor_copy(
                    nrm[:], p.bank(3, 2, 512)), waits=(m2,))
                p.done_bank(3, v3)
                v4 = p.emit("scalar", lambda e: e.activation(
                    nrm[:], nrm[:], AF.Sqrt), waits=(v3,))
                v5 = p.emit("vector", lambda e: e.tensor_single_scalar(
                    nrm[:], nrm[:], 1e-12, ALU.max), waits=(v4,))
                v6 = p.emit("vector", lambda e: e.reciprocal(
                    nrm[:], nrm[:]), waits=(v5,))
                m3 = p.emit("tensor", lambda e: e.matmul(
                    p.bank(3, 128, 512), bo2[:], nrm[:], start=True,
                    stop=True), waits=(v6,) + p.use_bank(3), tinc=True)
                v7 = p.emit("vector", lambda e: e.tensor_copy(
                    nrb[:], p.bank(3, 128, 512)), waits=(m3,))
                p.done_bank(3, v7)
                evac = p.emit("vector", lambda e, o=o, t=t: e.tensor_mul(
                    ob[o % 2][:, t * 512:(t + 1) * 512], qraw[:], nrb[:]),
                    waits=(v7,))
            store[(tag, o)] = p.dma("sync", dst_d[o], ob[o % 2][:],
                                    tag + "_st", waits=(evac,))
            ob_guard[o % 2] = store[(tag, o)]

    qk_norm_proj("q", wq, qT_d)
    qk_norm_proj("k", wk, kT_d)

    for o in range(8):
        evac = None
        for t in range(8):
            bi = 4 + (o * 8 + t) % 2
            ps = p.bank(bi)
            w0 = p.use_bank(bi) + (wld, xnp)
            if t == 0 and (o % 2) in ob_guard:
                w0 = w0 + (ob_guard[o % 2],)
            mm = None
            for k in range(8):
                mm = p.emit("tensor",
                            lambda e, ps=ps, k=k, o=o, t=t: e.matmul(
                                ps, wv[k][:, o * 128:(o + 1) * 128],
                                xn[k][:, t * 512:(t + 1) * 512],
                                start=(k == 0), stop=(k == 7)),
                            waits=w0 if k == 0 else (), tinc=(k == 7))
            evac = p.emit("vector", lambda e, ps=ps, o=o, t=t: e.tensor_copy(
                ob[o % 2][:, t * 512:(t + 1) * 512], ps), waits=(mm,))
            p.done_bank(bi, evac)
        store[("v", o)] = p.dma("sync", vT_d[o], ob[o % 2][:], "v_st",
                                waits=(evac,))
        ob_guard[o % 2] = store[("v", o)]

    larow = p.pool.alloc("larow", [128, T], F32)
    for o in range(8):
        lasc = None
        for t in range(8):
            bi = 6 + (o * 8 + t) % 2
            ps = p.bank(bi)
            w0 = p.use_bank(bi) + (wld, xnp, cols_pred)
            if t == 0 and ("la", o - 1) in store:
                w0 = w0 + (store[("la", o - 1)],)
            mm = None
            for k in range(8):
                mm = p.emit("tensor",
                            lambda e, ps=ps, k=k, o=o, t=t: e.matmul(
                                ps, wgp[k][:, o * 128:(o + 1) * 128],
                                xn[k][:, t * 512:(t + 1) * 512],
                                start=(k == 0), stop=(k == 7)),
                            waits=w0 if k == 0 else (), tinc=(k == 7))
            s1 = p.emit("scalar", lambda e, ps=ps, o=o, t=t: e.activation(
                larow[:, t * 512:(t + 1) * 512], ps, AF.Sigmoid,
                bias=sv["bg"][:, o:o + 1]), waits=(mm, lasc)
                if lasc else (mm,))
            p.done_bank(bi, s1)
            v1 = p.emit("vector", lambda e, t=t: e.tensor_single_scalar(
                larow[:, t * 512:(t + 1) * 512],
                larow[:, t * 512:(t + 1) * 512], 1e-6, ALU.max),
                waits=(s1,))
            s2 = p.emit("scalar", lambda e, t=t: e.activation(
                larow[:, t * 512:(t + 1) * 512],
                larow[:, t * 512:(t + 1) * 512], AF.Ln), waits=(v1,))
            for n in range(8):
                lasc = p.emit("vector",
                              lambda e, t=t, n=n: e.tensor_tensor_scan(
                                  larow[:, t * 512 + n * 64:
                                        t * 512 + (n + 1) * 64],
                                  larow[:, t * 512 + n * 64:
                                        t * 512 + (n + 1) * 64],
                                  larow[:, t * 512 + n * 64:
                                        t * 512 + (n + 1) * 64],
                                  0.0, ALU.add, ALU.bypass), waits=(s2,))
        store[("la", o)] = p.dma("sync", la_d[o], larow[:], "la_st",
                                 waits=(lasc,))
    p.free_later(*wq, *wk, *wv, *wgp, *xn, *ob, qraw, qsq, nrm, nrb, larow)
    p.barrier()

    S = p.pool.alloc("S", [128, 128], F32)
    kBD = p.pool.alloc("kBD", [128, 128], BF16)
    vBD = p.pool.alloc("vBD", [128, 128], BF16)
    zb0 = p.emit("vector", lambda e: e.memset(kBD[:], 0.0))
    zb1 = p.emit("vector", lambda e: e.memset(vBD[:], 0.0))
    lam = p.pool.alloc("lam", [2, T], F32)
    eb = p.pool.alloc("eb", [128, T], F32)
    qp = p.pool.alloc("qp", [128, T], BF16)
    kp = p.pool.alloc("kp", [128, T], BF16)
    qdec = p.pool.alloc("qdec", [128, T], BF16)
    Sb = p.pool.alloc("Sb", [128, 128], BF16)
    cdall = p.pool.alloc("cdall", [128, 64], F32)
    k2c = p.pool.alloc("k2c", [128, 64], BF16)
    sTm = p.pool.alloc_n("sTm", 2, [128, 64], BF16)
    k2t = p.pool.alloc("k2t", [64, 128], BF16)
    vtt = p.pool.alloc("vtt", [64, 128], BF16)
    kvm = p.pool.alloc("kvm", [128, 128], F32)
    obf = p.pool.alloc("obf", [128, T], BF16)
    tm1 = p.pool.alloc("tm1", [128, T], F32)

    qs = RowStream(p, lambda j: qT_d[j], T, BF16, queue="sync", depth=2,
                   name="qrs", nrows=8)
    ks = RowStream(p, lambda j: kT_d[j], T, BF16, queue="sync", depth=2,
                   name="krs", nrows=8)
    vs = RowStream(p, lambda j: vT_d[j], T, BF16, queue="sync", depth=2,
                   name="vrs", nrows=8)
    las = RowStream(p, lambda j: la_d[j], T, F32, queue="sync", depth=2,
                    name="lars", nrows=8)
    ost = {}
    for j in range(8):
        qrow, qw = qs.get(j)
        krow, kw = ks.get(j)
        vrow, vw = vs.get(j)
        larow2, lw = las.get(j)
        wl = tuple(qw) + tuple(kw) + tuple(vw) + tuple(lw) + \
            ((store[("la", 7)]),) + cst_w
        if j >= 2:
            wl = wl + (ost[j - 2],)
        lmm = None
        for t in range(8):
            lmm = p.emit("tensor", lambda e, larow2=larow2, t=t: e.matmul(
                p.bank(5, 2, 512), bonesf[:],
                larow2[:, t * 512:(t + 1) * 512], start=True, stop=True),
                waits=wl + p.use_bank(5), tinc=True)
            lc = p.emit("vector", lambda e, t=t: e.tensor_copy(
                lam[:, t * 512:(t + 1) * 512], p.bank(5, 2, 512)),
                waits=(lmm,))
            p.done_bank(5, lc)
            lmm = lc
        ep1 = p.emit("vector", lambda e: e.tensor_scalar_mul(
            lam[:], lam[:], 1.0 / 64), waits=(lmm,))
        bcp = None
        for t in range(8):
            bmm = p.emit("tensor", lambda e, t=t: e.matmul(
                p.bank(5, 128, 512), bo2[:], lam[:, t * 512:(t + 1) * 512],
                start=True, stop=True), waits=(ep1,) + p.use_bank(5),
                tinc=True)
            bcp = p.emit("vector", lambda e, t=t: e.tensor_copy(
                eb[:, t * 512:(t + 1) * 512], p.bank(5, 128, 512)),
                waits=(bmm,))
            p.done_bank(5, bcp)
        eqp = p.emit("scalar", lambda e: e.activation(
            tm1[:], eb[:], AF.Exp), waits=(bcp,))
        q1 = p.emit("vector", lambda e, qrow=qrow: e.tensor_mul(
            qp[:], qrow[:], tm1[:]), waits=(eqp,))
        qs.freed_when(q1)
        ekp = p.emit("scalar", lambda e: e.activation(
            tm1[:], eb[:], AF.Exp, scale=-1.0), waits=(q1,))
        k1 = p.emit("vector", lambda e, krow=krow: e.tensor_mul(
            kp[:], krow[:], tm1[:]), waits=(ekp,))
        ed1 = p.emit("scalar", lambda e, larow2=larow2: e.activation(
            tm1[:], larow2[:], AF.Exp), waits=(k1,))
        qd1 = p.emit("vector", lambda e, qrow=qrow: e.tensor_mul(
            qdec[:], qrow[:], tm1[:]), waits=(ed1,))
        cd1 = p.emit("scalar", lambda e, larow2=larow2: e.activation(
            cdall[:], larow2[:].rearrange("p (n c) -> p n c", c=64)
            [:, :, 63], AF.Exp), waits=(qd1,))
        sz = p.emit("vector", lambda e: e.memset(S[:], 0.0),
                    waits=(ost.get(j - 1, ("V", 0)),) if j > 0 else ())
        prevo = None
        prevsT = {}
        su_prev = sz
        prev_t1 = None
        for n in range(64):
            cs = slice(n * 64, (n + 1) * 64)
            a1_ = p.emit("vector", lambda e, cs=cs: e.tensor_copy(
                kBD[0:64, 0:64], kp[0:64, cs]),
                waits=(k1, zb0) if n == 0 else (prevsT[0],))
            a2_ = p.emit("vector", lambda e, cs=cs: e.tensor_copy(
                kBD[64:128, 64:128], kp[64:128, cs]), waits=(a1_,))
            smm = p.emit("tensor", lambda e, cs=cs: e.matmul(
                p.bank(3, 128, 64), kBD[:], qp[:, cs], start=True,
                stop=True), waits=(a2_, q1) + p.use_bank(3), tinc=True)
            ms = p.emit("vector", lambda e, n=n: e.tensor_mul(
                sTm[n % 2][:], p.bank(3, 128, 64), maskT[:]),
                waits=(smm,))
            p.done_bank(3, ms)
            prevsT[0] = ms
            wc1 = (ms,) if prev_t1 is None else (ms, prev_t1)
            c1_ = p.emit("vector",
                         lambda e, cs=cs, n=n, larow2=larow2:
                         e.tensor_scalar(
                             k2c[:], larow2[:, cs],
                             larow2[:, n * 64 + 63:n * 64 + 64],
                             None, ALU.subtract), waits=wc1)
            c2_ = p.emit("scalar", lambda e: e.activation(
                k2c[:], k2c[:], AF.Exp, scale=-1.0), waits=(c1_,))
            c3_ = p.emit("vector", lambda e, cs=cs, krow=krow: e.tensor_mul(
                k2c[:], k2c[:], krow[:, cs]), waits=(c2_,))
            t1_ = p.emit("tensor", lambda e: e.transpose(
                p.bank(5, 64, 64).bitcast(BF16), k2c[:], identb[:]),
                waits=(c3_,) + p.use_bank(5), tinc=True)
            prev_t1 = t1_
            t2_ = p.emit("vector", lambda e: e.tensor_copy(
                k2t[:], p.bank(5, 64, 64).bitcast(BF16)), waits=(t1_,))
            p.done_bank(5, t2_)
            t3_ = p.emit("tensor",
                         lambda e, cs=cs, vrow=vrow: e.transpose(
                             p.bank(5, 64, 64).bitcast(BF16), vrow[:, cs],
                             identb[:]),
                         waits=(t2_,) + p.use_bank(5), tinc=True)
            t4_ = p.emit("vector", lambda e: e.tensor_copy(
                vtt[:], p.bank(5, 64, 64).bitcast(BF16)), waits=(t3_,))
            p.done_bank(5, t4_)
            b1_ = p.emit("vector", lambda e: e.tensor_copy(
                vBD[0:64, 0:64], vtt[:, 0:64]),
                waits=(t4_, zb1) + ((prevo,) if prevo else ()))
            b2_ = p.emit("vector", lambda e: e.tensor_copy(
                vBD[64:128, 64:128], vtt[:, 64:128]), waits=(b1_,))
            obk = p.bank(4, 128, 64)
            om1 = p.emit("tensor", lambda e, n=n: e.matmul(
                obk, vBD[:], sTm[n % 2][:], start=True, stop=False),
                waits=(b2_, ms) + p.use_bank(4))
            kmm = p.emit("tensor", lambda e: e.matmul(
                p.bank(6, 128, 128), k2t[:], vtt[:], start=True,
                stop=True), waits=(t4_,) + p.use_bank(6), tinc=True)
            km1 = p.emit("vector", lambda e: e.tensor_mul(
                kvm[:], p.bank(6, 128, 128), blkm[:]), waits=(kmm,))
            p.done_bank(6, km1)
            sb_ = p.emit("vector", lambda e: e.tensor_copy(Sb[:], S[:]),
                         waits=(su_prev,))
            om2 = p.emit("tensor", lambda e, cs=cs: e.matmul(
                obk, Sb[:], qdec[:, cs], start=False, stop=True),
                waits=(sb_, qd1), tinc=True)
            su = p.emit("vector", lambda e, n=n: e.scalar_tensor_tensor(
                S[:], S[:], cdall[:, n:n + 1], kvm[:], ALU.mult, ALU.add),
                waits=(km1, om2, cd1))
            su_prev = su
            oc = p.emit("vector", lambda e, cs=cs: e.tensor_copy(
                obf[:, cs], obk), waits=(om2,))
            p.done_bank(4, oc)
            prevo = oc
        ks.freed_when(k1, c3_)
        vs.freed_when(t4_)
        las.freed_when(su)
        ost[j] = p.dma("sync", oT_d[j], obf[:], "o_st", waits=(prevo,))
    qs.done(), ks.done(), vs.done(), las.done()
    p.free_later(S, Sb, kBD, vBD, lam, eb, qp, kp, qdec, cdall, k2c, *sTm,
                 k2t, vtt, kvm, obf, tm1)
    p.barrier()

    wo = p.pool.alloc_n("wo_", 8, [128, 1024], BF16)
    wold = None
    for k in range(8):
        wold = p.dma("gpsimd", wo[k][:], wo_d[k], "wo_ld")
    mvs = RowStream(p, lambda i: oT_d[i % 8][:, (i // 8) * 512:
                                             (i // 8 + 1) * 512],
                    512, BF16, queue="sync", depth=4, name="woPm", nrows=64)
    ys3 = RowStream(p, lambda i: yd[i % 8][:, (i // 8) * 512:
                                           (i // 8 + 1) * 512],
                    512, F32, queue="gpsimd", depth=4, name="woPy", nrows=64)
    xib = p.pool.alloc_n("xib", 4, [128, 512], F32)
    xist = {}

    def statW(k, o):
        return wo[k][:, o * 128:(o + 1) * 128], (wold,)

    def movW(k, t):
        buf, w = mvs.get(t * 8 + k)
        return buf[:], w, mvs.freed_when

    def epilW(o, t, ps, gmm):
        i = t * 8 + o
        ybuf, yw = ys3.get(i)
        yi = i % 4
        w = (gmm,) + tuple(yw)
        if i >= 4:
            w = w + (xist[i - 4],)
        f1 = p.emit("vector", lambda e, yi=yi, ps=ps, ybuf=ybuf:
                    e.tensor_add(xib[yi][:], ps, ybuf[:]), waits=w)
        ys3.freed_when(f1)
        xist[i] = p.dma("sync", xi_d[o][:, t * 512:(t + 1) * 512],
                        xib[yi][:], "xi_st", waits=(f1,))
        return f1

    proj_kmajor(p, "woP", 8, 8, statW, movW, epilW)
    p.free_later(*wo, *xib)
    p.barrier()

    rows = p.pool.alloc_n("r2", 4, [1, T], F32)
    a1b, a2b, mu2, r2 = rows
    tmr = p.pool.alloc("tmr2", [1, T], F32)
    var2 = p.pool.alloc("var2", [1, T], F32)
    xis = RowStream(p, lambda k: xi_d[k], T, F32, queue="sync", depth=2,
                    name="xis", nrows=8)

    def xirow(k):
        buf, w = xis.get(k)
        return buf, tuple(w) + (xist[63],)
    sp2 = ln_stats_stream(p, "st2", consts, 8, xirow, a1b, a2b, stream=xis)
    xis.done()
    lp2 = ln_rows(p, "lr2", a1b, a2b, D, (sp2,), mu2, r2, var2, tmr)
    mub2 = p.pool.alloc("mub2", [128, T], F32)
    rb2 = p.pool.alloc("rb2", [128, T], F32)
    bc4 = bcast_row(p, "bcD", mu2, mub2, consts, (lp2,))
    bc5 = bcast_row(p, "bcE", r2, rb2, consts, (lp2,))
    p.free_later(*rows, tmr, var2)
    p.barrier()
    xn2 = p.pool.alloc_n("xn2_", 8, [128, T], BF16)
    xis2 = RowStream(p, lambda k: xi_d[k], T, F32, queue="sync", depth=2,
                     name="xis2", nrows=8)
    tmt2 = p.pool.alloc("tmt2", [128, T], F32)
    xn2p = None
    for k in range(8):
        buf, w = xis2.get(k)
        w = tuple(w) + (bc4, bc5, cols_pred)
        if xn2p is not None:
            w = w + (xn2p,)
        e1 = p.emit("vector", lambda e, buf=buf: e.tensor_sub(
            tmt2[:], buf[:], mub2[:]), waits=w)
        xis2.freed_when(e1)
        e2 = p.emit("vector", lambda e: e.tensor_mul(tmt2[:], tmt2[:],
                                                     rb2[:]), waits=(e1,))
        xn2p = p.emit("vector", lambda e, k=k: e.tensor_scalar(
            xn2[k][:], tmt2[:], sv["ln2w"][:, k:k + 1],
            sv["ln2b"][:, k:k + 1], ALU.mult, ALU.add), waits=(e2,))
    xis2.done()
    p.free_later(tmt2, mub2, rb2)
    p.barrier()

    w1r = p.pool.alloc_n("w1r_", 8, [128, 4096], BF16)
    w1ld = None
    for k in range(8):
        w1ld = p.dma("gpsimd", w1r[k][:], w1_d[k], "w1_ld")
    obh = p.pool.alloc_n("obh", 2, [128, T], BF16)
    hst = {}
    for o in range(32):
        evac = None
        for t in range(8):
            bi = (o * 8 + t) % 6
            ps = p.bank(bi)
            w0 = p.use_bank(bi) + (w1ld, xn2p)
            if t == 0 and o >= 2:
                w0 = w0 + (hst[o - 2],)
            mm = None
            for k in range(8):
                mm = p.emit("tensor",
                            lambda e, ps=ps, k=k, o=o, t=t: e.matmul(
                                ps, w1r[k][:, o * 128:(o + 1) * 128],
                                xn2[k][:, t * 512:(t + 1) * 512],
                                start=(k == 0), stop=(k == 7)),
                            waits=w0 if k == 0 else (), tinc=(k == 7))
            evac = p.emit("scalar", lambda e, ps=ps, o=o, t=t: e.activation(
                obh[o % 2][:, t * 512:(t + 1) * 512], ps, AF.Gelu,
                bias=sv["b1"][:, o:o + 1]), waits=(mm, cols_pred))
            p.done_bank(bi, evac)
        hst[o] = p.dma("sync", h_d[o], obh[o % 2][:], "h_st", waits=(evac,))
    p.free_later(*w1r, *xn2, *obh)
    p.barrier()

    w2r = p.pool.alloc_n("w2r_", 32, [128, 1024], BF16)
    w2ld = None
    for k in range(32):
        w2ld = p.dma("gpsimd", w2r[k][:], w2_d[k], "w2_ld")
    mvs2 = RowStream(p, lambda i: h_d[i % 32][:, (i // 32) * 512:
                                              (i // 32 + 1) * 512],
                     512, BF16, queue="sync", depth=4, name="p7mv",
                     nrows=256)
    xs4 = RowStream(p, lambda i: x_d[i % 8][:, (i // 8) * 512:
                                            (i // 8 + 1) * 512],
                    512, F32, queue="gpsimd", depth=4, name="p7xs",
                    nrows=64)
    xis4 = RowStream(p, lambda i: xi_d[i % 8][:, (i // 8) * 512:
                                              (i // 8 + 1) * 512],
                     512, F32, queue="gpsimd", depth=4, name="p7xi",
                     nrows=64)
    yb = p.pool.alloc_n("ybF", 4, [128, 512], F32)
    ystF = {}

    def stat7(k, o):
        return w2r[k][:, o * 128:(o + 1) * 128], (w2ld,)

    def mov7(k, t):
        buf, w = mvs2.get(t * 32 + k)
        return buf[:], w, mvs2.freed_when

    def epil7(o, t, ps, gmm):
        i = t * 8 + o
        xbuf, xw = xs4.get(i)
        xibuf, xiw = xis4.get(i)
        yi = i % 4
        w = (gmm, cols_pred) + tuple(xw) + tuple(xiw)
        if i >= 4:
            w = w + (ystF[i - 4],)
        f1 = p.emit("vector", lambda e, o=o, yi=yi, ps=ps:
                    e.tensor_scalar_add(yb[yi][:], ps,
                                        sv["b2"][:, o:o + 1]), waits=w)
        f2 = p.emit("vector", lambda e, yi=yi, xibuf=xibuf: e.tensor_add(
            yb[yi][:], yb[yi][:], xibuf[:]), waits=(f1,))
        xis4.freed_when(f2)
        f3 = p.emit("vector", lambda e, yi=yi, xbuf=xbuf: e.tensor_add(
            yb[yi][:], yb[yi][:], xbuf[:]), waits=(f2,))
        xs4.freed_when(f3)
        ystF[i] = p.dma("sync", y_d[o][:, t * 512:(t + 1) * 512],
                        yb[yi][:], "y_st", waits=(f3,))
        return f3

    proj_kmajor(p, "p7", 32, 8, stat7, mov7, epil7)
    if debug:
        dbg = {}
        for nm, src, shp, dt in (("d_qT", qT_d, [8, 128, T], BF16),
                                 ("d_kT", kT_d, [8, 128, T], BF16),
                                 ("d_vT", vT_d, [8, 128, T], BF16),
                                 ("d_la", la_d, [8, 128, T], F32),
                                 ("d_oT", oT_d, [8, 128, T], BF16),
                                 ("d_xi", xi_d, [8, 128, T], F32),
                                 ("d_y0", yd, [8, 128, T], F32)):
            t = nc.dram_tensor(nm, shp, dt, kind="ExternalOutput").ap()
            for k in range(8):
                p.dma("gpsimd", t[k], src[k], "dbg_st",
                      waits=(ystF[63],))
        p.emit_wait("gpsimd", (("dbg_st", p.cnt["dbg_st"]),))
    p.emit_wait("sync", (ystF[63],))
    p.emit_wait("gpsimd", (ystF[63],))
    p.finalize()
    return nc


def _l0attn_inputs(xc, Wq, Wk, Wv, Wg, bg, Wo, ln1w, ln1b, ln2w, ln2b,
                   W1, b1, W2, b2):
    bones = np.zeros((128, 2), np.float32)
    bones[0:64, 0] = 1.0
    bones[64:128, 1] = 1.0
    bo2 = np.ascontiguousarray(bones.T)
    maskT = (np.arange(128)[:, None] % 64 <= np.arange(64)[None, :]) \
        .astype(np.float32)
    blkm = np.zeros((128, 128), np.float32)
    blkm[0:64, 0:64] = 1.0
    blkm[64:128, 64:128] = 1.0
    return {
        "x": trows(np.ascontiguousarray(np.asarray(xc, np.float32).T)),
        "wqT": trows(np.ascontiguousarray(np.asarray(Wq, np.float32))),
        "wkT": trows(np.ascontiguousarray(np.asarray(Wk, np.float32))),
        "wvT": trows(np.ascontiguousarray(np.asarray(Wv, np.float32))),
        "wgpT": trows(np.ascontiguousarray(np.asarray(Wg, np.float32))),
        "woT": trows(np.ascontiguousarray(np.asarray(Wo, np.float32))),
        "w1T": trows(np.ascontiguousarray(np.asarray(W1, np.float32))),
        "w2T": trows(np.ascontiguousarray(np.asarray(W2, np.float32))),
        "bg": colform(bg), "ln1w": colform(ln1w), "ln1b": colform(ln1b),
        "ln2w": colform(ln2w), "ln2b": colform(ln2b),
        "b1": colform(b1), "b2": colform(b2),
        "bones": bones, "bo2": bo2, "maskT": maskT, "blkm": blkm,
    }


def run_l0_attn(x, Wq, Wk, Wv, Wg, bg, Wo, ln1w, ln1b, ln2w, ln2b, W1, b1,
                W2, b2, n_cores=8):
    nc = bass.Bass("TRN2", target_bir_lowering=False, debug=False)
    build_l0_attn(nc)
    in_maps = [_l0attn_inputs(np.ascontiguousarray(x[i]), Wq, Wk, Wv, Wg,
                              bg, Wo, ln1w, ln1b, ln2w, ln2b, W1, b1, W2,
                              b2) for i in range(n_cores)]
    res = run_bass_kernel_spmd(nc, in_maps, core_ids=list(range(n_cores)))
    outs = [np.asarray(r["y"]).reshape(1024, T).T for r in res.results]
    return np.stack(outs, 0)


def build_l3_attn(nc):
    p = Prog(nc)
    x_d = nc.dram_tensor("x", [8, 128, T], F32, kind="ExternalInput").ap()
    wts = {}
    for nm in ("wq", "wk", "wv", "wgt", "wb", "wo"):
        wts[nm] = nc.dram_tensor(nm + "T", [8, 128, 1024], F32,
                                 kind="ExternalInput").ap()
    wa_d = nc.dram_tensor("waT", [8, 128, 16], F32,
                          kind="ExternalInput").ap()
    alog_d = nc.dram_tensor("alog", [1, 16], F32, kind="ExternalInput").ap()
    dtb_d = nc.dram_tensor("dtb", [1, 16], F32, kind="ExternalInput").ap()
    nw_d = nc.dram_tensor("nwc", [128, 1], F32, kind="ExternalInput").ap()
    bo_d = nc.dram_tensor("bo", [128, 8], F32, kind="ExternalInput").ap()
    selJP_d = nc.dram_tensor("selJP", [8, 16, 128], F32,
                             kind="ExternalInput").ap()
    sel8_d = nc.dram_tensor("sel8", [16, 128], F32,
                            kind="ExternalInput").ap()
    moff_d = nc.dram_tensor("moff", [128, 64], F32,
                            kind="ExternalInput").ap()
    strict_d = nc.dram_tensor("strict", [128, 64], F32,
                              kind="ExternalInput").ap()
    blkm_d = nc.dram_tensor("blkm", [128, 128], F32,
                            kind="ExternalInput").ap()
    bones_d = nc.dram_tensor("bones", [128, 2], F32,
                             kind="ExternalInput").ap()
    bo2_d = nc.dram_tensor("bo2", [2, 128], F32, kind="ExternalInput").ap()
    y_d = nc.dram_tensor("y", [8, 128, T], F32, kind="ExternalOutput").ap()

    qT_d = nc.dram_tensor("qTD", [8, 128, T], BF16, kind="Internal").ap()
    kT_d = nc.dram_tensor("kTD", [8, 128, T], BF16, kind="Internal").ap()
    vT_d = nc.dram_tensor("vTD", [8, 128, T], BF16, kind="Internal").ap()
    gT_d = nc.dram_tensor("gTD", [8, 128, T], BF16, kind="Internal").ap()
    betaD = nc.dram_tensor("betaD", [16, 64, T], F32, kind="Internal").ap()
    lacD = nc.dram_tensor("lacD", [16, T], F32, kind="Internal").ap()
    rD = nc.dram_tensor("rD", [16, 64, 64], F32, kind="Internal").ap()
    dD = nc.dram_tensor("dDD", [16, 64, 64], F32, kind="Internal").ap()
    mD = nc.dram_tensor("mDD", [16, 64, 64], F32, kind="Internal").ap()
    ys_d = nc.dram_tensor("ysD", [8, 128, T], F32, kind="Internal").ap()
    ctx_d = nc.dram_tensor("ctxD", [8, 128, T], BF16, kind="Internal").ap()

    consts = Consts(p)
    p.barrier()
    cst = {}
    cpred = None
    for nm, dram, shp, dt in (("selJP", selJP_d, [8, 16, 128], F32),
                              ("sel8", sel8_d, [16, 128], F32),
                              ("moff", moff_d, [128, 64], F32),
                              ("strict", strict_d, [128, 64], F32),
                              ("blkm", blkm_d, [128, 128], F32),
                              ("nwc", nw_d, [128, 1], F32),
                              ("bo", bo_d, [128, 8], F32),
                              ("alog", alog_d, [1, 16], F32),
                              ("dtb", dtb_d, [1, 16], F32)):
        if nm == "selJP":
            cst[nm] = p.pool.alloc(nm, [16, 8 * 128], F32)
            cpred = p.dma("sync", cst[nm][:].rearrange(
                "p (j f) -> p j f", f=128), dram[:].rearrange(
                "j p f -> p j f"), "cst_ld")
        else:
            cst[nm] = p.pool.alloc(nm, shp, dt)
            cpred = p.dma("sync", cst[nm][:], dram[:], "cst_ld")
    identb = p.pool.alloc("identb", [128, 128], BF16)
    identf = p.pool.alloc("identf2", [128, 128], F32)
    i0 = p.emit("gpsimd", lambda e: e.memset(identb[:], 1.0))
    i1 = p.emit("gpsimd", lambda e: e.affine_select(
        identb[:], identb[:], [[1, 128]], ALU.is_equal, 0.0,
        channel_multiplier=-1), waits=(i0,))
    i2 = p.emit("gpsimd", lambda e: e.memset(identf[:], 1.0))
    i3 = p.emit("gpsimd", lambda e: e.affine_select(
        identf[:], identf[:], [[1, 128]], ALU.is_equal, 0.0,
        channel_multiplier=-1), waits=(i2,))
    negx = p.pool.alloc("negx", [1, 16], F32)
    dtbb = p.pool.alloc("dtbb", [128, 16], F32)
    n1 = p.emit("scalar", lambda e: e.activation(negx[:], cst["alog"][:],
                                                 AF.Exp), waits=(cpred,))
    n2 = p.emit("vector", lambda e: e.tensor_scalar_mul(negx[:], negx[:],
                                                        -1.0), waits=(n1,))
    n3 = p.emit("tensor", lambda e: e.matmul(
        p.bank(0, 128, 16), consts.ones_row_f[:], cst["dtb"][:],
        start=True, stop=True), waits=(cpred,), tinc=True)
    n4 = p.emit("vector", lambda e: e.tensor_copy(dtbb[:], p.bank(0, 128,
                                                                  16)),
                waits=(n3,))
    p.done_bank(0, n4)

    u, up = ln_u(p, consts, x_d, "lnD")
    p.barrier()

    wrow = {}
    wld = None
    for nm in ("wq", "wk", "wv", "wgt", "wb"):
        wrow[nm] = p.pool.alloc_n(nm + "_", 8, [128, 1024], BF16)
        for k in range(8):
            wld = p.dma("gpsimd", wrow[nm][k][:], wts[nm][k], "w_ld")
    wa = p.pool.alloc("wa_", [128, 8 * 16], BF16)
    wav = wa[:].rearrange("p (k h) -> p k h", h=16)
    wld = p.dma("gpsimd", wav, wa_d.rearrange("k p h -> p k h"), "w_ld")

    qraw = p.pool.alloc("qraw", [128, 512], F32)
    qsq = p.pool.alloc("qsq", [128, 512], BF16)
    nrm = p.pool.alloc("nrm", [2, 512], F32)
    nrb = p.pool.alloc("nrb", [128, 512], F32)
    ob = p.pool.alloc_n("obp", 2, [128, T], BF16)
    obF = p.pool.alloc("obF", [128, T], F32)
    bones = p.pool.alloc("bones", [128, 2], BF16)
    bo2f = p.pool.alloc("bo2f", [2, 128], F32)
    bz3 = p.dma("gpsimd", bones[:], bones_d[:], "cst_ld2")
    bz6 = p.dma("sync", bo2f[:], bo2_d[:], "cst_ld")
    cst_w = (bz3, bz6, i1, i3, cpred, n2, n4)
    store = {}
    ob_guard = {}
    qsqg = [None]

    def qk_proj(tag, wrows, dst_d, qscale):
        for o in range(8):
            evac = None
            for t in range(8):
                bi = (o * 8 + t) % 3
                ps = p.bank(bi)
                w0 = p.use_bank(bi) + (wld, up) + cst_w
                if t == 0 and (o % 2) in ob_guard:
                    w0 = w0 + (ob_guard[o % 2],)
                mm = None
                for k in range(8):
                    mm = p.emit("tensor",
                                lambda e, ps=ps, k=k, o=o, t=t,
                                wrows=wrows: e.matmul(
                                    ps, wrows[k][:, o * 128:(o + 1) * 128],
                                    u[k][:, t * 512:(t + 1) * 512],
                                    start=(k == 0), stop=(k == 7)),
                                waits=w0 if k == 0 else (), tinc=(k == 7))
                v1 = p.emit("vector", lambda e, ps=ps: e.tensor_copy(
                    qraw[:], ps), waits=(mm, evac) if evac else (mm,))
                p.done_bank(bi, v1)
                wv2 = (v1,) if qsqg[0] is None else (v1, qsqg[0])
                v2 = p.emit("scalar", lambda e: e.activation(
                    qsq[:], qraw[:], AF.Square), waits=wv2)
                m2 = p.emit("tensor", lambda e: e.matmul(
                    p.bank(3, 2, 512), bones[:], qsq[:], start=True,
                    stop=True), waits=(v2,) + p.use_bank(3), tinc=True)
                qsqg[0] = m2
                v3 = p.emit("vector", lambda e: e.tensor_copy(
                    nrm[:], p.bank(3, 2, 512)), waits=(m2,))
                p.done_bank(3, v3)
                v4 = p.emit("scalar", lambda e: e.activation(
                    nrm[:], nrm[:], AF.Sqrt), waits=(v3,))
                v6 = p.emit("vector", lambda e: e.reciprocal(
                    nrm[:], nrm[:]), waits=(v4,))
                if qscale != 1.0:
                    v6 = p.emit("vector", lambda e: e.tensor_scalar_mul(
                        nrm[:], nrm[:], qscale), waits=(v6,))
                m3 = p.emit("tensor", lambda e: e.matmul(
                    p.bank(3, 128, 512), bo2f[:], nrm[:], start=True,
                    stop=True), waits=(v6,) + p.use_bank(3), tinc=True)
                v7 = p.emit("vector", lambda e: e.tensor_copy(
                    nrb[:], p.bank(3, 128, 512)), waits=(m3,))
                p.done_bank(3, v7)
                evac = p.emit("vector", lambda e, o=o, t=t: e.tensor_mul(
                    ob[o % 2][:, t * 512:(t + 1) * 512], qraw[:], nrb[:]),
                    waits=(v7,))
            store[(tag, o)] = p.dma("sync", dst_d[o], ob[o % 2][:],
                                    tag + "_st", waits=(evac,))
            ob_guard[o % 2] = store[(tag, o)]

    qk_proj("q", wrow["wq"], qT_d, 0.125)
    qk_proj("k", wrow["wk"], kT_d, 1.0)

    def plain_proj(tag, wrows, dst_d, act=None, banks=(4, 5)):
        for o in range(8):
            evac = None
            for t in range(8):
                bi = banks[(o * 8 + t) % 2]
                ps = p.bank(bi)
                w0 = p.use_bank(bi) + (wld, up)
                if t == 0 and (tag, o - 2) in store:
                    w0 = w0 + (store[(tag, o - 2)],)
                mm = None
                for k in range(8):
                    mm = p.emit("tensor",
                                lambda e, ps=ps, k=k, o=o, t=t,
                                wrows=wrows: e.matmul(
                                    ps, wrows[k][:, o * 128:(o + 1) * 128],
                                    u[k][:, t * 512:(t + 1) * 512],
                                    start=(k == 0), stop=(k == 7)),
                                waits=w0 if k == 0 else (), tinc=(k == 7))
                if act is None:
                    evac = p.emit("vector",
                                  lambda e, ps=ps, o=o, t=t: e.tensor_copy(
                                      ob[o % 2][:, t * 512:(t + 1) * 512],
                                      ps), waits=(mm,))
                else:
                    wac = (mm,) + ((evac,) if evac else ())
                    if t == 0 and (tag, o - 1) in store:
                        wac = wac + (store[(tag, o - 1)],)
                    evac = p.emit("scalar",
                                  lambda e, ps=ps, o=o, t=t: e.activation(
                                      obF[:, t * 512:(t + 1) * 512], ps,
                                      act), waits=wac)
                p.done_bank(bi, evac)
            if act is None:
                store[(tag, o)] = p.dma("sync", dst_d[o], ob[o % 2][:],
                                        tag + "_st", waits=(evac,))
            else:
                p.dma("sync", dst_d[2 * o], obF[0:64, :],
                      tag + "_st", waits=(evac,))
                store[(tag, o)] = p.dma("sync", dst_d[2 * o + 1],
                                        obF[64:128, :], tag + "_st")
        return store[(tag, 7)]

    plain_proj("v", wrow["wv"], vT_d)
    plain_proj("g", wrow["wgt"], gT_d)
    plain_proj("b", wrow["wb"], betaD, act=AF.Sigmoid, banks=(6, 7))
    p.free_later(*[wrow[nm][k] for nm in wrow for k in range(8)],
                 qraw, qsq, nrm, nrb, *ob, obF)
    p.barrier()

    laH = p.pool.alloc("laH", [16, T], F32)
    zt = p.pool.alloc("zt", [128, 16], F32)
    zsp = p.pool.alloc("zsp", [128, 16], F32)
    lprev = None
    for tt in range(32):
        mm = None
        for k in range(8):
            w0 = (wld, up) + (p.use_bank(0) if k == 0 else ())
            mm = p.emit("tensor", lambda e, k=k, tt=tt: e.matmul(
                p.bank(0, 128, 16), u[k][:, tt * 128:(tt + 1) * 128],
                wav[:, k, :], start=(k == 0), stop=(k == 7)),
                waits=w0 if k == 0 else (), tinc=(k == 7))
        z1 = p.emit("vector", lambda e: e.tensor_add(
            zt[:], p.bank(0, 128, 16), dtbb[:]),
            waits=(mm, cst_w[4], n4) + ((lprev,) if lprev else ()))
        p.done_bank(0, z1)
        z2 = p.emit("scalar", lambda e: e.activation(zsp[:], zt[:], AF.Abs),
                    waits=(z1,))
        z3 = p.emit("scalar", lambda e: e.activation(zsp[:], zsp[:], AF.Exp,
                                                     scale=-1.0),
                    waits=(z2,))
        z4 = p.emit("scalar", lambda e: e.activation(zsp[:], zsp[:], AF.Ln,
                                                     bias=1.0), waits=(z3,))
        z5 = p.emit("vector", lambda e: e.tensor_single_scalar(
            zt[:], zt[:], 0.0, ALU.max), waits=(z4,))
        z6 = p.emit("vector", lambda e: e.tensor_add(zt[:], zt[:], zsp[:]),
                    waits=(z5,))
        m2 = p.emit("tensor", lambda e: e.matmul(
            p.bank(1, 128, 16), consts.ones_row_f[:], negx[:],
            start=True, stop=True), waits=(n2,) + p.use_bank(1), tinc=True)
        z7 = p.emit("vector", lambda e: e.tensor_copy(zsp[:],
                                                      p.bank(1, 128, 16)),
                    waits=(m2,))
        p.done_bank(1, z7)
        z8 = p.emit("vector", lambda e: e.tensor_mul(zt[:], zt[:], zsp[:]),
                    waits=(z6, z7))
        tm = p.emit("tensor", lambda e: e.transpose(
            p.bank(2, 16, 128), zt[:],
            identf[:]), waits=(z8, i3) + p.use_bank(2), tinc=True)
        lprev = p.emit("vector", lambda e, tt=tt: e.tensor_copy(
            laH[:, tt * 128:(tt + 1) * 128], p.bank(2, 16, 128)),
            waits=(tm,))
        p.done_bank(2, lprev)
    lsc = lprev
    for n in range(64):
        lsc = p.emit("vector", lambda e, n=n: e.tensor_tensor_scan(
            laH[:, n * 64:(n + 1) * 64], laH[:, n * 64:(n + 1) * 64],
            laH[:, n * 64:(n + 1) * 64], 0.0, ALU.add, ALU.bypass),
            waits=(lsc,))
    lacst = p.dma("sync", lacD[:], laH[:], "lac_st", waits=(lsc,))
    p.free_later(wa, laH, zt, zsp)
    p.barrier()

    S8 = p.pool.alloc_n("S8_", 8, [128, 128], F32)
    Sb = p.pool.alloc("Sbb", [128, 128], BF16)
    qw = p.pool.alloc("qw", [128, T], BF16)
    kw = p.pool.alloc("kw", [128, T], BF16)
    vw = p.pool.alloc("vw", [128, T], BF16)
    lacw = p.pool.alloc("lacw", [16, 512], F32)
    lacT = p.pool.alloc("lacT", [64, 16], F32)
    lacJ = p.pool.alloc("lacJ", [128, 1], F32)
    DTm = p.pool.alloc("DTm", [128, 64], F32)
    abc = p.pool.alloc("abc", [128, 64], F32)
    kBD = p.pool.alloc("kBD2", [128, 128], BF16)
    kz = p.emit("vector", lambda e: e.memset(kBD[:], 0.0))
    dBD = p.pool.alloc("dBD", [128, 128], BF16)
    dz = p.emit("vector", lambda e: e.memset(dBD[:], 0.0))
    t128 = p.pool.alloc("t128", [128, 64], F32)
    rP = p.pool.alloc("rP", [128, 64], F32)
    MtP = p.pool.alloc("MtP", [128, 64], F32)
    Mti = p.pool.alloc("Mti", [64, 128], F32)
    dTt = p.pool.alloc("dTt", [64, 128], BF16)
    k2P = p.pool.alloc("k2P", [128, 64], BF16)
    k2t2 = p.pool.alloc("k2t2", [64, 128], BF16)
    yPb = p.pool.alloc_n("yPb", 2, [128, 64], F32)
    mC = p.pool.alloc("mC", [16, 4096], F32)
    Msol = p.pool.alloc("Msol", [128, 4096], F32)
    rS = p.pool.alloc("rS", [128, 512], F32)
    bS = p.pool.alloc("bS", [128, 512], F32)
    dS = p.pool.alloc("dS", [128, 512], F32)
    cS = p.pool.alloc("cS", [128, 8], F32)
    tS = p.pool.alloc("tS", [128, 512], F32)
    sz = None
    for jp in range(8):
        sz = p.emit("vector", lambda e, jp=jp: e.memset(S8[jp][:], 0.0))
    szp = sz

    yst = {}
    ypb_guard = {}
    dstore = {}
    rstore = {}
    mstore = {}
    solp = None
    for w in range(8):
        wpred = []
        for jp in range(8):
            wpred.append(p.dma("sync", qw[:, jp * 512:(jp + 1) * 512],
                               qT_d[jp][:, w * 512:(w + 1) * 512], "qw_ld",
                               waits=(store[("q", 7)], szp)))
            wpred.append(p.dma("sync", kw[:, jp * 512:(jp + 1) * 512],
                               kT_d[jp][:, w * 512:(w + 1) * 512], "qw_ld",
                               waits=(store[("k", 7)],)))
            wpred.append(p.dma("sync", vw[:, jp * 512:(jp + 1) * 512],
                               vT_d[jp][:, w * 512:(w + 1) * 512], "qw_ld",
                               waits=(store[("v", 7)],)))
        lacp = p.dma("sync", lacw[:], lacD[:, w * 512:(w + 1) * 512],
                     "qw_ld", waits=(lacst,))
        wl = (("qw_ld", p.cnt["qw_ld"]),)
        for nw in range(8):
            n = w * 8 + nw
            ns = slice(nw * 64, (nw + 1) * 64)
            tmm = p.emit("tensor", lambda e, ns=ns: e.transpose(
                p.bank(7, 64, 16), lacw[:, ns], identf[0:16, 0:16]),
                waits=wl + p.use_bank(7) + (i3,), tinc=True)
            lt = p.emit("vector", lambda e: e.tensor_copy(
                lacT[:], p.bank(7, 64, 16)), waits=(tmm,))
            p.done_bank(7, lt)
            for jp in range(8):
                js = slice(jp * 512 + nw * 64, jp * 512 + (nw + 1) * 64)
                t1m = p.emit("tensor", lambda e, jp=jp, ns=ns: e.matmul(
                    p.bank(0, 128, 64), cst["selJP"][:, jp * 128:
                                                     (jp + 1) * 128],
                    lacw[:, ns], start=True, stop=True),
                    waits=wl + p.use_bank(0) + (cpred,), tinc=True)
                l1 = p.emit("vector", lambda e, jp=jp: e.tensor_copy(
                    lacJ[0:64, :], lacT[:, 2 * jp:2 * jp + 1]),
                    waits=(lt, t1m))
                l2 = p.emit("vector", lambda e, jp=jp: e.tensor_copy(
                    lacJ[64:128, :], lacT[:, 2 * jp + 1:2 * jp + 2]),
                    waits=(l1,))
                a1 = p.emit("scalar", lambda e: e.activation(
                    abc[:], p.bank(0, 128, 64), AF.Exp), waits=(t1m, l2))
                d1 = p.emit("vector", lambda e: e.tensor_scalar(
                    t128[:], p.bank(0, 128, 64), lacJ[:, 0:1], None,
                    ALU.subtract), waits=(a1,))
                p.done_bank(0, d1)
                d2 = p.emit("vector", lambda e: e.tensor_sub(
                    t128[:], t128[:], cst["moff"][:]), waits=(d1,))
                d3 = p.emit("scalar", lambda e: e.activation(
                    DTm[:], t128[:], AF.Exp), waits=(d2,))
                sb = p.emit("vector", lambda e, jp=jp: e.tensor_copy(
                    Sb[:], S8[jp][:]), waits=(szp, d3))
                km = p.emit("tensor", lambda e, js=js: e.matmul(
                    p.bank(1, 128, 64), Sb[:], kw[:, js], start=True,
                    stop=True), waits=(sb,) + p.use_bank(1), tinc=True)
                r1 = p.emit("vector", lambda e: e.tensor_mul(
                    rP[:], abc[:], p.bank(1, 128, 64)), waits=(km,))
                p.done_bank(1, r1)
                r2 = p.emit("vector", lambda e, js=js: e.tensor_sub(
                    rP[:], vw[:, js],
                    rP[:]), waits=(r1,))
                rstore[(n, jp)] = p.dma(
                    "gpsimd", rD[2 * jp:2 * jp + 2].rearrange(
                        "hh e j -> (hh e) j"), rP[:], "r_st", waits=(r2,))
                kb1 = p.emit("vector", lambda e, js=js: e.tensor_copy(
                    kBD[0:64, 0:64], kw[0:64, js]),
                    waits=(kz, szp) + ((mstore[(n - 1, jp)],)
                                       if (n - 1, jp) in mstore else ()))
                kb2 = p.emit("vector", lambda e, js=js: e.tensor_copy(
                    kBD[64:128, 64:128], kw[64:128, js]), waits=(kb1,))
                skm = p.emit("tensor", lambda e, js=js: e.matmul(
                    p.bank(3, 128, 64), kBD[:], kw[:, js], start=True,
                    stop=True), waits=(kb2,) + p.use_bank(3), tinc=True)
                mk1 = p.emit("vector", lambda e: e.tensor_mul(
                    MtP[:], p.bank(3, 128, 64), DTm[:]), waits=(skm, d3))
                p.done_bank(3, mk1)
                mk2 = p.emit("vector", lambda e: e.tensor_mul(
                    MtP[:], MtP[:], cst["strict"][:]), waits=(mk1,))
                tmm2 = p.emit("tensor", lambda e: e.transpose(
                    p.bank(4, 64, 128), MtP[:], identf[:]),
                    waits=(mk2,) + p.use_bank(4), tinc=True)
                mv1 = p.emit("vector", lambda e: e.tensor_copy(
                    Mti[:], p.bank(4, 64, 128)), waits=(tmm2,))
                p.done_bank(4, mv1)
                mstore[(n, jp)] = p.dma(
                    "gpsimd", mD[2 * jp:2 * jp + 2].rearrange(
                        "hh i j -> i hh j"), Mti[:].rearrange(
                        "i (hh j) -> i hh j", j=64), "m_st", waits=(mv1,))
            rld = p.dma("sync", rS[:].rearrange(
                "p (es j) -> p es j", j=64),
                rD[:].rearrange("h (eg es) j -> (h eg) es j", eg=8),
                "sol_ld", waits=tuple(rstore[(n, jp)] for jp in range(8)))
            bld = p.dma("sync", bS[:].rearrange(
                "p (es j) -> p es j", j=64),
                betaD[:, :, n * 64:(n + 1) * 64].rearrange(
                    "h (eg es) j -> (h eg) es j", eg=8),
                "sol_ld", waits=(store[("b", 7)],))
            mld = p.dma("sync", mC[:],
                        mD[:].rearrange("h i j -> h (i j)"), "sol_ld",
                        waits=tuple(mstore[(n, jp)] for jp in range(8)))
            mrep = None
            for q8 in range(8):
                rm = p.emit("tensor", lambda e, q8=q8: e.matmul(
                    p.bank(5), cst["sel8"][:], mC[:, q8 * 512:
                                                  (q8 + 1) * 512],
                    start=True, stop=True),
                    waits=(mld, cpred) + p.use_bank(5), tinc=True)
                mrep = p.emit("vector", lambda e, q8=q8: e.tensor_copy(
                    Msol[:, q8 * 512:(q8 + 1) * 512], p.bank(5)),
                    waits=(rm,))
                p.done_bank(5, mrep)
            dSv = dS[:].rearrange("p (es j) -> p es j", j=64)
            rSv = rS[:].rearrange("p (es j) -> p es j", j=64)
            bSv = bS[:].rearrange("p (es j) -> p es j", j=64)
            s0 = p.emit("vector", lambda e: e.scalar_tensor_tensor(
                dS[:], rS[:], 1.0, bS[:], ALU.mult, ALU.mult),
                waits=(rld, bld, solp) if solp else (rld, bld))
            prev = s0
            for i in range(1, 64):
                mrow = bass.AP(Msol, i * 64,
                               [[4096, 128], [0, 8], [1, i]])
                t1v = p.emit("vector", lambda e, i=i, mrow=mrow:
                             e.tensor_tensor(tS[:].rearrange(
                                 "p (es j) -> p es j", j=64)[:, :, 0:i],
                                 dSv[:, :, 0:i], mrow, ALU.mult),
                             waits=(prev, mrep))
                t2v = p.emit("vector", lambda e, i=i: e.tensor_reduce(
                    cS[:], tS[:].rearrange("p (es j) -> p es j",
                                           j=64)[:, :, 0:i], AX.X,
                    ALU.add), waits=(t1v,))
                t3v = p.emit("vector", lambda e, i=i: e.tensor_sub(
                    cS[:], rSv[:, :, i], cS[:]), waits=(t2v,))
                prev = p.emit("vector", lambda e, i=i: e.tensor_mul(
                    dSv[:, :, i], bSv[:, :, i], cS[:]), waits=(t3v,))
            dstore[n] = p.dma("gpsimd", dD[:].rearrange(
                "h (eg es) j -> (h eg) es j", eg=8),
                dS[:].rearrange("p (es j) -> p es j", j=64),
                "d_st", waits=(prev,))
            solp = prev
            for jp in range(8):
                js = slice(jp * 512 + nw * 64, jp * 512 + (nw + 1) * 64)
                t1m = p.emit("tensor", lambda e, jp=jp, ns=ns: e.matmul(
                    p.bank(0, 128, 64), cst["selJP"][:, jp * 128:
                                                     (jp + 1) * 128],
                    lacw[:, ns], start=True, stop=True),
                    waits=p.use_bank(0), tinc=True)
                l1 = p.emit("vector", lambda e, jp=jp: e.tensor_copy(
                    lacJ[0:64, :], lacT[:, 2 * jp:2 * jp + 1]),
                    waits=(lt, t1m))
                l2 = p.emit("vector", lambda e, jp=jp: e.tensor_copy(
                    lacJ[64:128, :], lacT[:, 2 * jp + 1:2 * jp + 2]),
                    waits=(l1,))
                a1 = p.emit("scalar", lambda e: e.activation(
                    abc[:], p.bank(0, 128, 64), AF.Exp), waits=(t1m, l2))
                d1 = p.emit("vector", lambda e: e.tensor_scalar(
                    t128[:], p.bank(0, 128, 64), lacJ[:, 0:1], None,
                    ALU.subtract), waits=(a1,))
                p.done_bank(0, d1)
                d2 = p.emit("vector", lambda e: e.tensor_sub(
                    t128[:], t128[:], cst["moff"][:]), waits=(d1,))
                d3 = p.emit("scalar", lambda e: e.activation(
                    DTm[:], t128[:], AF.Exp), waits=(d2,))
                dls = p.dma("sync", rP[:], dD[2 * jp:2 * jp + 2].rearrange(
                    "hh e j -> (hh e) j"), "dp_ld", waits=(dstore[n],))
                tdm = p.emit("tensor", lambda e: e.transpose(
                    p.bank(4, 64, 128), rP[:], identf[:]),
                    waits=(dls,) + p.use_bank(4), tinc=True)
                dt1 = p.emit("vector", lambda e: e.tensor_copy(
                    dTt[:], p.bank(4, 64, 128)), waits=(tdm,))
                p.done_bank(4, dt1)
                db1 = p.emit("vector", lambda e: e.tensor_copy(
                    dBD[0:64, 0:64], dTt[:, 0:64]), waits=(dt1, dz))
                db2 = p.emit("vector", lambda e: e.tensor_copy(
                    dBD[64:128, 64:128], dTt[:, 64:128]), waits=(db1,))
                kb1 = p.emit("vector", lambda e, js=js: e.tensor_copy(
                    kBD[0:64, 0:64], kw[0:64, js]),
                    waits=(kz, dstore[n]))
                kb2 = p.emit("vector", lambda e, js=js: e.tensor_copy(
                    kBD[64:128, 64:128], kw[64:128, js]), waits=(kb1,))
                sqm = p.emit("tensor", lambda e, js=js: e.matmul(
                    p.bank(3, 128, 64), kBD[:], qw[:, js], start=True,
                    stop=True), waits=(kb2,) + p.use_bank(3), tinc=True)
                sq1 = p.emit("vector", lambda e: e.tensor_mul(
                    t128[:], p.bank(3, 128, 64), DTm[:]), waits=(sqm, d3))
                p.done_bank(3, sq1)
                sq2 = p.emit("vector", lambda e: e.tensor_copy(
                    k2P[:], t128[:]), waits=(sq1,))
                ym = p.emit("tensor", lambda e: e.matmul(
                    p.bank(2, 128, 64), dBD[:], k2P[:], start=True,
                    stop=True), waits=(db2, sq2) + p.use_bank(2),
                    tinc=True)
                sb = p.emit("vector", lambda e, jp=jp: e.tensor_copy(
                    Sb[:], S8[jp][:]), waits=(szp,))
                qm = p.emit("tensor", lambda e, js=js: e.matmul(
                    p.bank(1, 128, 64), Sb[:], qw[:, js], start=True,
                    stop=True), waits=(sb,) + p.use_bank(1), tinc=True)
                yb_ = yPb[jp % 2]
                y1 = p.emit("vector", lambda e, yb_=yb_: e.tensor_mul(
                    yb_[:], abc[:], p.bank(1, 128, 64)),
                    waits=(qm, ypb_guard.get(jp % 2, ("V", 0))))
                p.done_bank(1, y1)
                y2 = p.emit("vector", lambda e, yb_=yb_: e.tensor_add(
                    yb_[:], yb_[:], p.bank(2, 128, 64)), waits=(y1, ym))
                p.done_bank(2, y2)
                yst[(n, jp)] = p.dma("sync", ys_d[jp][:, n * 64:
                                                      (n + 1) * 64],
                                     yb_[:], "ys_st", waits=(y2,))
                ypb_guard[jp % 2] = yst[(n, jp)]
                k21 = p.emit("vector", lambda e, js=js: e.tensor_scalar_mul(
                    k2P[:], kw[:, js], DTm[:, 63:64]), waits=(y2,))
                tkm = p.emit("tensor", lambda e: e.transpose(
                    p.bank(4, 64, 64).bitcast(BF16), k2P[:], identb[:]),
                    waits=(k21, i1) + p.use_bank(4), tinc=True)
                k22 = p.emit("vector", lambda e: e.tensor_copy(
                    k2t2[:], p.bank(4, 64, 64).bitcast(BF16)),
                    waits=(tkm,))
                p.done_bank(4, k22)
                dsm = p.emit("tensor", lambda e: e.matmul(
                    p.bank(5, 128, 128), k2t2[:], dTt[:], start=True,
                    stop=True), waits=(k22,) + p.use_bank(5), tinc=True)
                su1 = p.emit("vector", lambda e: e.tensor_mul(
                    tS[:, 0:128], p.bank(5, 128, 128), cst["blkm"][:]),
                    waits=(dsm,))
                p.done_bank(5, su1)
                szp = p.emit("vector", lambda e, jp=jp: e.scalar_tensor_tensor(
                    S8[jp][:], S8[jp][:], abc[:, 63:64], tS[:, 0:128],
                    ALU.mult, ALU.add), waits=(su1, y2))
    p.free_later(Sb, qw, kw, vw, lacw, lacT, lacJ, DTm, abc, kBD, dBD, t128,
                 rP, MtP, Mti, dTt, k2P, k2t2, *yPb, mC, Msol, rS, bS, dS,
                 cS, tS, *S8)
    p.barrier()

    ysr = RowStream(p, lambda j: ys_d[j], T, F32, queue="sync", depth=2,
                    name="ysr", nrows=8)
    gr = RowStream(p, lambda j: gT_d[j], T, BF16, queue="sync", depth=2,
                   name="grs", nrows=8)
    ctxb = p.pool.alloc_n("ctxb", 2, [128, T], BF16)
    sqb = p.pool.alloc("sqb", [128, 512], BF16)
    nr2 = p.pool.alloc("nr2", [2, 512], F32)
    nrb2 = p.pool.alloc("nrb2", [128, 512], F32)
    silg = p.pool.alloc("silg", [128, 512], F32)
    cst_done = {}
    prevq = None
    for j in range(8):
        yrow, ywt = ysr.get(j)
        grow, gwt = gr.get(j)
        last = None
        for t in range(8):
            tsl = slice(t * 512, (t + 1) * 512)
            w0 = tuple(ywt) + tuple(gwt) + (("ys_st",
                                             p.cnt["ys_st"]),)
            if prevq is not None:
                w0 = w0 + (prevq,)
            s1 = p.emit("scalar", lambda e, yrow=yrow, tsl=tsl:
                        e.activation(sqb[:], yrow[:, tsl], AF.Square),
                        waits=w0)
            m1 = p.emit("tensor", lambda e: e.matmul(
                p.bank(6, 2, 512), bones[:], sqb[:], start=True,
                stop=True), waits=(s1,) + p.use_bank(6), tinc=True)
            v1 = p.emit("vector", lambda e: e.tensor_copy(
                nr2[:], p.bank(6, 2, 512)), waits=(m1,))
            p.done_bank(6, v1)
            prevq = m1
            v2 = p.emit("scalar", lambda e: e.activation(
                nr2[:], nr2[:], AF.Ln, bias=1e-6, scale=1.0 / 64),
                waits=(v1,))
            v3 = p.emit("scalar", lambda e: e.activation(
                nr2[:], nr2[:], AF.Exp, scale=-0.5), waits=(v2,))
            m2 = p.emit("tensor", lambda e: e.matmul(
                p.bank(6, 128, 512), bo2f[:], nr2[:], start=True,
                stop=True), waits=(v3,) + p.use_bank(6), tinc=True)
            v4 = p.emit("vector", lambda e: e.tensor_copy(
                nrb2[:], p.bank(6, 128, 512)), waits=(m2,))
            p.done_bank(6, v4)
            s2 = p.emit("scalar", lambda e, grow=grow, tsl=tsl:
                        e.activation(silg[:], grow[:, tsl], AF.Silu),
                        waits=tuple(gwt))
            c1 = p.emit("vector", lambda e, yrow=yrow, tsl=tsl:
                        e.tensor_mul(nrb2[:], nrb2[:], yrow[:, tsl]),
                        waits=(v4,))
            c2 = p.emit("vector", lambda e: e.tensor_scalar_mul(
                nrb2[:], nrb2[:], cst["nwc"][:]), waits=(c1,))
            last = p.emit("vector", lambda e, j=j, tsl=tsl: e.tensor_mul(
                ctxb[j % 2][:, tsl], nrb2[:], silg[:]), waits=(c2, s2))
        ysr.freed_when(last)
        gr.freed_when(last)
        cst_done[j] = p.dma("sync", ctx_d[j], ctxb[j % 2][:], "ctx_st",
                            waits=(last,))
    ysr.done()
    gr.done()
    p.free_later(*ctxb, sqb, nr2, nrb2, silg)
    p.barrier()

    wo = p.pool.alloc_n("woD_", 8, [128, 1024], BF16)
    wold = None
    for k in range(8):
        wold = p.dma("gpsimd", wo[k][:], wts["wo"][k], "wo_ld")
    mvs = RowStream(p, lambda i: ctx_d[i % 8][:, (i // 8) * 512:
                                              (i // 8 + 1) * 512],
                    512, BF16, queue="sync", depth=4, name="oDm", nrows=64)
    xs3 = RowStream(p, lambda i: x_d[i % 8][:, (i // 8) * 512:
                                            (i // 8 + 1) * 512],
                    512, F32, queue="gpsimd", depth=4, name="oDx", nrows=64)
    yb2 = p.pool.alloc_n("yb2", 4, [128, 512], F32)
    ystF = {}

    def statO(k, o):
        return wo[k][:, o * 128:(o + 1) * 128], (wold, cst_done[7])

    def movO(k, t):
        buf, w = mvs.get(t * 8 + k)
        return buf[:], w, mvs.freed_when

    def epilO(o, t, ps, gmm):
        i = t * 8 + o
        xbuf, xw = xs3.get(i)
        yi = i % 4
        w = (gmm, cpred) + tuple(xw)
        if i >= 4:
            w = w + (ystF[i - 4],)
        f1 = p.emit("vector", lambda e, o=o, yi=yi, ps=ps:
                    e.tensor_scalar_add(yb2[yi][:], ps,
                                        cst["bo"][:, o:o + 1]), waits=w)
        f2 = p.emit("vector", lambda e, yi=yi, xbuf=xbuf: e.tensor_add(
            yb2[yi][:], yb2[yi][:], xbuf[:]), waits=(f1,))
        xs3.freed_when(f2)
        ystF[i] = p.dma("sync", y_d[o][:, t * 512:(t + 1) * 512],
                        yb2[yi][:], "y_st", waits=(f2,))
        return f2

    proj_kmajor(p, "oD", 8, 8, statO, movO, epilO)
    p.emit_wait("sync", (ystF[63],))
    p.emit_wait("gpsimd", (ystF[63],))
    p.finalize()
    return nc


def _l3attn_inputs(xc, Wq, Wk, Wv, Wg, Wb, Wa, dt_bias, A_log, norm_w,
                   Wo, bo):
    selJP = np.zeros((8, 16, 128), np.float32)
    for jp in range(8):
        selJP[jp, 2 * jp, 0:64] = 1.0
        selJP[jp, 2 * jp + 1, 64:128] = 1.0
    sel8 = np.zeros((16, 128), np.float32)
    for h in range(16):
        sel8[h, h * 8:(h + 1) * 8] = 1.0
    pos = np.arange(128)[:, None] % 64
    col = np.arange(64)[None, :]
    moff = ((pos > col) * 1e9).astype(np.float32)
    strict = (pos < col).astype(np.float32)
    blkm = np.zeros((128, 128), np.float32)
    blkm[0:64, 0:64] = 1.0
    blkm[64:128, 64:128] = 1.0
    bones = np.zeros((128, 2), np.float32)
    bones[0:64, 0] = 1.0
    bones[64:128, 1] = 1.0
    nwc = np.tile(np.asarray(norm_w, np.float32), 2).reshape(128, 1)
    return {
        "x": trows(np.ascontiguousarray(np.asarray(xc, np.float32).T)),
        "wqT": trows(np.ascontiguousarray(np.asarray(Wq, np.float32).T)),
        "wkT": trows(np.ascontiguousarray(np.asarray(Wk, np.float32).T)),
        "wvT": trows(np.ascontiguousarray(np.asarray(Wv, np.float32).T)),
        "wgtT": trows(np.ascontiguousarray(np.asarray(Wg, np.float32).T)),
        "wbT": trows(np.ascontiguousarray(np.asarray(Wb, np.float32).T)),
        "woT": trows(np.ascontiguousarray(np.asarray(Wo, np.float32).T)),
        "waT": trows(np.ascontiguousarray(np.asarray(Wa, np.float32).T)),
        "alog": np.asarray(A_log, np.float32).reshape(1, 16),
        "dtb": np.asarray(dt_bias, np.float32).reshape(1, 16),
        "nwc": nwc, "bo": colform(bo),
        "selJP": selJP, "sel8": sel8, "moff": moff, "strict": strict,
        "blkm": blkm, "bones": bones, "bo2": np.ascontiguousarray(bones.T),
    }


def run_l3_attn(x, Wq, Wk, Wv, Wg, Wb, Wa, dt_bias, A_log, norm_w, Wo, bo,
                n_cores=8):
    nc = bass.Bass("TRN2", target_bir_lowering=False, debug=False)
    build_l3_attn(nc)
    in_maps = [_l3attn_inputs(np.ascontiguousarray(x[i]), Wq, Wk, Wv, Wg,
                              Wb, Wa, dt_bias, A_log, norm_w, Wo, bo)
               for i in range(n_cores)]
    res = run_bass_kernel_spmd(nc, in_maps, core_ids=list(range(n_cores)))
    outs = [np.asarray(r["y"]).reshape(1024, T).T for r in res.results]
    return np.stack(outs, 0)


def _l3ffn_inputs(xc, w1, b1, ln1g, ln1b, ln2g, ln2b, w2, b2):
    return {
        "x": trows(np.ascontiguousarray(np.asarray(xc, np.float32).T)),
        "w1T": trows(np.ascontiguousarray(np.asarray(w1, np.float32).T)),
        "w2T": trows(np.ascontiguousarray(np.asarray(w2, np.float32).T)),
        "ln1g": colform(ln1g), "ln1b": colform(ln1b),
        "b1": colform(b1), "ln2g": colform(ln2g), "ln2b": colform(ln2b),
        "b2": colform(b2),
    }


def run_l3_ffn(x, w1, b1, ln1g, ln1b, ln2g, ln2b, w2, b2, n_cores=8):
    nc = bass.Bass("TRN2", target_bir_lowering=False, debug=False)
    build_l3_ffn(nc)
    in_maps = [_l3ffn_inputs(np.ascontiguousarray(x[i]), w1, b1, ln1g, ln1b,
                             ln2g, ln2b, w2, b2) for i in range(n_cores)]
    res = run_bass_kernel_spmd(nc, in_maps, core_ids=list(range(n_cores)))
    outs = [np.asarray(r["y"]).reshape(1024, T).T for r in res.results]
    return np.stack(outs, 0)


def kernel(**inputs):
    g = lambda n: np.asarray(inputs[n], np.float32)
    x = g("L0_attn_x")
    x = run_l0_attn(x, g("L0_attn_Wq"), g("L0_attn_Wk"), g("L0_attn_Wv"),
                    g("L0_attn_Wg"), g("L0_attn_bg"), g("L0_attn_Wo"),
                    g("L0_attn_ln1_w"), g("L0_attn_ln1_b"),
                    g("L0_attn_ln2_w"), g("L0_attn_ln2_b"),
                    g("L0_attn_W1"), g("L0_attn_b1"), g("L0_attn_W2"),
                    g("L0_attn_b2"))
    x = run_l0_ffn(x, g("L0_ffn_w1_w"), g("L0_ffn_w1_s"), g("L0_ffn_w2_w"),
                   g("L0_ffn_w2_s"), g("L0_ffn_h1_w"), g("L0_ffn_h1_s"),
                   g("L0_ffn_h2_w"), g("L0_ffn_h2_s"), g("L0_ffn_w3_w"),
                   g("L0_ffn_w3_s"))
    x = run_l1_attn(x, g("L1_attn_W_Q"), g("L1_attn_W_K"), g("L1_attn_W_V"),
                    g("L1_attn_W_out"), g("L1_attn_b_out"))
    x = run_moe(x, g("L1_ffn_Wg").T, g("L1_ffn_W1"), g("L1_ffn_b1"),
                g("L1_ffn_W2"), g("L1_ffn_b2"), None, False, False, 8602)
    x = run_l2_attn(x, g("L2_attn_attn_w"), g("L2_attn_attn_b"),
                    g("L2_attn_out_w"), g("L2_attn_out_b"))
    x = run_moe(x, g("L2_ffn_Wg"), g("L2_ffn_W1"), g("L2_ffn_b1"),
                g("L2_ffn_W2"), g("L2_ffn_b2"), g("L2_ffn_bg"), True, True,
                9831)
    x = run_l3_attn(x, g("L3_attn_Wq"), g("L3_attn_Wk"), g("L3_attn_Wv"),
                    g("L3_attn_Wg"), g("L3_attn_Wb"), g("L3_attn_Wa"),
                    g("L3_attn_dt_bias"), g("L3_attn_A_log"),
                    g("L3_attn_norm_w"), g("L3_attn_Wo"), g("L3_attn_bo"))
    x = run_l3_ffn(x, g("L3_ffn_w1"), g("L3_ffn_b1"), g("L3_ffn_ln1_g"),
                   g("L3_ffn_ln1_b"), g("L3_ffn_ln2_g"), g("L3_ffn_ln2_b"),
                   g("L3_ffn_w2"), g("L3_ffn_b2"))
    return np.ascontiguousarray(x.astype(np.float32))
